# version 03
# baseline (speedup 1.0000x reference)
import numpy as np
import ml_dtypes
import concourse.bass as bass
import concourse.mybir as mybir
from concourse.bass_utils import run_bass_kernel_spmd

F32 = mybir.dt.float32
BF16 = mybir.dt.bfloat16
I8 = mybir.dt.int8
I16 = mybir.dt.int16
AF = mybir.ActivationFunctionType
OP = mybir.AluOpType
AX = mybir.AxisListType

D = 2048
IC = 16
H = 16
TOWN = 1024
TKV = 2048
I_FFN = 5632
ICF = 44
SCALE = 1.0 / np.sqrt(128.0)
SOFTCAP = 30.0
EPS_LN = 1e-5
EPS_Q = 1e-8
QMAX = 127.0
KB = 1024

ENGS = ("sy", "te", "ve", "sc", "gp")
DEBUG_TAPS = False


class Chan:
    def __init__(self, nc, name):
        self.ctx = nc.semaphore(name)
        self.sem = self.ctx.__enter__()
        self.n = 0
        self.emitted = 0

    def tag(self, instr, amt=1):
        instr.then_inc(self.sem, amt)
        self.emitted += amt


class Sched:
    def __init__(self):
        self.ops = {e: [] for e in ENGS}

    def on(self, eng, fn):
        self.ops[eng].append(fn)


class Alloc:

    def __init__(self, nc, total_bytes):
        self.nc = nc
        self.total = total_bytes
        self.backing_ctx = nc.sbuf_tensor("arena0", [128, total_bytes],
                                          mybir.dt.uint8)
        self.backing = self.backing_ctx.__enter__()
        self.base = nc.lookup_mloc(self.backing).addr
        self.live = []

    def t(self, name, shape, dtype, off, ph_lo, ph_hi):
        per_part = int(np.prod(shape[1:])) * mybir.dt.size(dtype)
        assert off % 32 == 0, name
        end = off + per_part
        assert end <= self.total, (name, end, self.total)
        for (n2, s2, e2, l2, h2) in self.live:
            if not (end <= s2 or e2 <= off or h2 < ph_lo or ph_hi < l2):
                raise AssertionError(f"sbuf overlap {name} vs {n2}")
        self.live.append((name, off, end, ph_lo, ph_hi))
        h = self.nc.virtual_sbuf_tensor(name, list(shape), dtype)
        self.nc._tensor(h.name, list(shape), dtype, type="SB")
        mloc = self.nc.lookup_mloc(h)
        mloc.allocated = True
        mloc.addr = self.base + off
        return h


def build_nc():
    nc = bass.Bass("TRN2", target_bir_lowering=False,
                   detect_race_conditions=False)

    xkv_h = nc.dram_tensor("xkv", [TKV, D], F32, kind="ExternalInput").ap()
    wq_h = nc.dram_tensor("wq", [D, D], F32, kind="ExternalInput").ap()
    wk_h = nc.dram_tensor("wk", [D, D], F32, kind="ExternalInput").ap()
    wv_h = nc.dram_tensor("wv", [D, D], F32, kind="ExternalInput").ap()
    wo_h = nc.dram_tensor("wo", [D, D], F32, kind="ExternalInput").ap()
    wgu_h = nc.dram_tensor("wgu", [2 * I_FFN, D], F32, kind="ExternalInput").ap()
    wd_h = nc.dram_tensor("wd", [D, I_FFN], F32, kind="ExternalInput").ap()
    wmod_h = nc.dram_tensor("wmod", [D, 6 * D], F32, kind="ExternalInput").ap()
    cv_h = nc.dram_tensor("cv", [128, IC], F32, kind="ExternalInput").ap()
    valid_h = nc.dram_tensor("valid", [128, IC], F32, kind="ExternalInput").ap()
    mdiag_h = nc.dram_tensor("mdiag", [128, 128], BF16, kind="ExternalInput").ap()
    medge_h = nc.dram_tensor("medge", [128, 128], BF16, kind="ExternalInput").ap()
    mstr_h = nc.dram_tensor("mstr", [128, 12, 512], BF16,
                            kind="ExternalInput").ap()
    out_h = nc.dram_tensor("out", [TOWN, D], F32, kind="ExternalOutput").ap()
    xmid_h = nc.dram_tensor("xmid", [TOWN, D], F32, kind="Internal").ap()
    modsc_h = nc.dram_tensor("modsc", [6, D], F32, kind="Internal").ap()
    lsc_h = nc.dram_tensor("lsc", [32, 512], F32, kind="Internal").ap()
    if DEBUG_TAPS:
        dbg_h1 = nc.dram_tensor("dbg_h1", [128, IC, TKV], BF16,
                                kind="ExternalOutput").ap()
        dbg_mod = nc.dram_tensor("dbg_mod", [6, D], F32,
                                 kind="ExternalOutput").ap()
        dbg_at = nc.dram_tensor("dbg_at", [128, H, TOWN], BF16,
                                kind="ExternalOutput").ap()
        dbg_xm = nc.dram_tensor("dbg_xm", [TOWN, D], F32,
                                kind="ExternalOutput").ap()
        dbg_xqf = nc.dram_tensor("dbg_xqf", [128, IC, TOWN], BF16,
                                 kind="ExternalOutput").ap()
        dbg_s4 = nc.dram_tensor("dbg_s4", [128, 32], F32,
                                kind="ExternalOutput").ap()

    al = Alloc(nc, 207 * KB)

    off0 = 0

    def sm(name, shape, dtype):
        nonlocal off0
        h = al.t(name, shape, dtype, off0, 0, 9)
        per_part = int(np.prod(shape[1:])) * mybir.dt.size(dtype)
        off0 += (per_part + 31) // 32 * 32
        return h

    onesb = sm("onesb", [128, 128], BF16)
    mdiag = sm("mdiag_s", [128, 128], BF16)
    medge = sm("medge_s", [128, 128], BF16)
    valid = sm("valid_s", [128, IC], F32)
    cvf = sm("cvf", [128, IC], F32)
    cvs = sm("cvs", [128, IC], F32)
    mstat = sm("mstat", [128, IC, 128], BF16)
    stat = sm("stat", [128, 16, 128], F32)
    kb1 = sm("kb1", [128, 128], F32)
    kb2 = sm("kb2", [128, 128], F32)
    sx = sm("sx", [128, 8], F32)
    sxinv = sm("sxinv", [128, 8], F32)
    sh2 = sm("sh2", [128, 8], F32)
    sh2inv = sm("sh2inv", [128, 8], F32)
    wsc = sm("wsc", [128, 128], F32)
    wsinv = sm("wsinv", [128, 128], F32)
    gascr = sm("gascr", [128, 512], F32)
    gascr2 = sm("gascr2", [128, 512], F32)
    assert off0 <= 20 * KB, off0

    t1 = al.t("t1", [128, 512], F32, 20 * KB, 0, 9)
    sgt = al.t("sgt", [128, 2, 512], BF16, 22 * KB, 3, 3)
    outscr = al.t("outscr", [128, 2, 512], F32, 24 * KB, 4, 4)
    xqb = al.t("xqb", [128, D], BF16, 28 * KB, 2, 2)
    wotl = al.t("wotl", [128, D], BF16, 28 * KB, 6, 6)
    xmb = al.t("xmb", [128, 2, 512], F32, 28 * KB, 4, 4)
    xqf = al.t("xqf", [128, IC, TOWN], BF16, 32 * KB, 2, 3)
    gb_t = al.t("gb", [128, D], F32, 32 * KB, 4, 4)
    qscr8 = al.t("qscr8", [128, I_FFN], I8, 40 * KB, 4, 4)
    tanhs = al.t("tanhs", [128, 2, 512], F32, 46 * KB, 1, 1)
    lnlb = al.t("lnlb", [128, 512], F32, 50 * KB, 1, 1)
    mstr = al.t("mstr", [128, 12, 512], BF16, 52 * KB, 0, 1)
    h2qb = al.t("h2qb", [128, I_FFN], BF16, 46 * KB, 4, 4)

    RA = 64 * KB
    h1f = al.t("h1f", [128, IC, TKV], BF16, RA, 0, 1)
    wml = al.t("wml", [128, 2, IC, 512], BF16, RA + 64 * KB, 0, 0)
    attnsb = al.t("attnsb", [128, H, TOWN], BF16, RA + 64 * KB, 1, 2)
    xown = al.t("xown", [128, 8, D], F32, RA, 2, 2)
    arena = al.t("arena", [128, 9, I_FFN], BF16, RA, 3, 4)

    RB = 163 * KB
    sa_t = al.t("sa", [128, D], F32, RB, 0, 0)
    ca1_t = al.t("ca1", [128, D], F32, RB + 8 * KB, 0, 0)
    xin2 = al.t("xin2", [128, 2, D], F32, RB + 16 * KB, 0, 0)
    sqout = al.t("sqout", [128, D], F32, RB + 32 * KB, 0, 0)
    h1tok = al.t("h1tok", [128, D], BF16, RB + 40 * KB, 0, 0)
    wqt = al.t("wqt", [128, IC, 128], BF16, RB, 1, 1)
    wkt = al.t("wkt", [128, IC, 128], BF16, RB + 4 * KB, 1, 1)
    wvt = al.t("wvt", [128, IC, 128], BF16, RB + 8 * KB, 1, 1)
    whload = al.t("whload", [128, D], BF16, RB + 12 * KB, 1, 1)
    ksb = al.t("ksb", [128, TKV], BF16, RB + 16 * KB, 1, 1)
    vsb = al.t("vsb", [128, TKV], BF16, RB + 20 * KB, 1, 1)
    vt = al.t("vt", [128, IC, 128], BF16, RB + 24 * KB, 1, 1)
    qsb = al.t("qsb", [128, TOWN], BF16, RB + 28 * KB, 1, 1)
    pt = al.t("pt", [128, 12, 512], BF16, RB + 30 * KB, 1, 1)
    ga_t = al.t("ga", [128, D], F32, RB, 2, 2)
    sb_t = al.t("sb", [128, D], F32, RB + 8 * KB, 2, 2)
    cb1_t = al.t("cb1", [128, D], F32, RB + 16 * KB, 2, 2)
    lbc = al.t("lbc", [128, 512], F32, RB + 24 * KB, 2, 2)
    rcn = al.t("rcn", [128, 512], F32, RB + 26 * KB, 2, 2)
    wot = al.t("wot", [128, IC, 4, 128], BF16, RB + 28 * KB, 2, 2)
    hln2 = al.t("hln2", [128, D], F32, RB + 28 * KB, 7, 7)
    xqi8 = al.t("xqi8", [128, D], I8, RB + 36 * KB, 7, 7)
    wload = al.t("wload", [128, 2, D], BF16, RB, 3, 3)
    wi8 = al.t("wi8", [128, 2, D], I8, RB + 8 * KB, 3, 3)
    w1tg = al.t("w1tg", [128, IC, 512], BF16, RB + 12 * KB, 3, 3)
    w1tu = al.t("w1tu", [128, IC, 512], BF16, RB + 28 * KB, 3, 3)
    stage3 = al.t("stage3", [128, 1536], F32, 26 * KB, 3, 3)
    w2t4 = al.t("w2t4", [128, ICF, 2, 128], BF16, RB, 4, 4)
    w2load = al.t("w2load", [128, I_FFN], BF16, RB + 22 * KB, 4, 4)
    stage4 = al.t("stage4", [128, 1792], F32, 57 * KB, 4, 4)

    psA_ctx = nc.psum_tensor("psA", [128, 2048], F32)
    psA = psA_ctx.__enter__()
    psB_ctx = nc.psum_tensor("psB", [128, 2048], F32)
    psB = psB_ctx.__enter__()

    sched = Sched()
    C = {n: Chan(nc, n) for n in
         ("sy", "gpd", "scd", "te", "ve", "sc", "gp", "out", "ph")}

    def emit_newton(dep, out, srcap, scr, tmp, rsqrt=False, iters=3):
        ch, val = dep

        def _(ve, ch=ch, val=val):
            ve.wait_ge(C[ch].sem, val)
            i = ve.memset(out, 0.0)
            C["ve"].tag(i)
        sched.on("ve", _)
        C["ve"].n += 1
        ve_z = C["ve"].n

        def _(sc, ch=ch, val=val, ve_z=ve_z):
            sc.wait_ge(C[ch].sem, val)
            sc.wait_ge(C["ve"].sem, ve_z)
            sc.activation(scr, srcap.bitcast(I16)[:, 1::2], AF.Identity)
            if rsqrt:
                sc.activation(scr, scr, AF.Identity, scale=-0.5,
                              bias=kb1[:, 0:1])
            else:
                sc.activation(scr, scr, AF.Identity, scale=-1.0,
                              bias=kb2[:, 0:1])
            i = sc.activation(out.bitcast(I16)[:, 1::2], scr, AF.Identity)
            C["sc"].tag(i)
        sched.on("sc", _)
        C["sc"].n += 1
        sc_seed = C["sc"].n

        def _(ve, sc_seed=sc_seed):
            ve.wait_ge(C["sc"].sem, sc_seed)
            i = None
            for _it in range(iters):
                if rsqrt:
                    ve.tensor_tensor(tmp, srcap, out, OP.mult)
                    ve.tensor_tensor(tmp, tmp, out, OP.mult)
                    ve.tensor_scalar(tmp, tmp, -0.5, 1.5, OP.mult, OP.add)
                    i = ve.tensor_tensor(out, out, tmp, OP.mult)
                else:
                    ve.tensor_tensor(tmp, srcap, out, OP.mult)
                    ve.tensor_scalar(tmp, tmp, -1.0, 2.0, OP.mult, OP.add)
                    i = ve.tensor_tensor(out, out, tmp, OP.mult)
            C["ve"].tag(i)
        sched.on("ve", _)
        C["ve"].n += 1
        return C["ve"].n

    def barrier():
        n_sy, n_gpd = C["sy"].n, C["gpd"].n
        n_scd, n_out = C["scd"].n, C["out"].n
        for e in ENGS:
            def mark(eng, e=e, n_sy=n_sy, n_gpd=n_gpd, n_scd=n_scd,
                     n_out=n_out):
                if e == "sy":
                    eng.wait_ge(C["sy"].sem, n_sy)
                    eng.sem_inc(C["ph"].sem, 1)
                    C["ph"].emitted += 1
                elif e == "gp":
                    eng.wait_ge(C["gpd"].sem, n_gpd)
                    C["ph"].tag(eng.nop(), 1)
                elif e == "sc":
                    eng.wait_ge(C["scd"].sem, n_scd)
                    eng.wait_ge(C["out"].sem, n_out)
                    C["ph"].tag(eng.nop(), 1)
                else:
                    C["ph"].tag(eng.nop(), 1)
            sched.on(e, mark)
        C["ph"].n += 5
        tgt = C["ph"].n
        for e in ENGS:
            sched.on(e, lambda eng, tgt=tgt: eng.wait_ge(C["ph"].sem, tgt))

    def SB_(b, w=128):
        return stat[:, b, 0:w]

    def SCOL(b):
        return stat[:, b, 0:1]

    def _(sy):
        for dst, src in ((mdiag, mdiag_h), (medge, medge_h),
                         (valid, valid_h), (cvf, cv_h), (mstr, mstr_h)):
            i = sy.dma_start(out=dst[:], in_=src[:])
            C["sy"].tag(i, 16)
    sched.on("sy", _)
    C["sy"].n += 80
    sy_small = C["sy"].n

    def _(ve):
        ve.memset(onesb[:], 1.0)
        ve.memset(kb1[:], 24375.0)
        ve.memset(kb2[:], 32499.0)
        ve.memset(stat[:], 1.0)
        i = ve.memset(wsc[:], 1.0)
        C["ve"].tag(i)
    sched.on("ve", _)
    C["ve"].n += 1

    def _(sc):
        sc.wait_ge(C["sy"].sem, sy_small)
        i = sc.activation(cvs[:], cvf[:], AF.Silu)
        C["sc"].tag(i)
    sched.on("sc", _)
    C["sc"].n += 1
    sc_cvs = C["sc"].n

    def _(ve):
        ve.wait_ge(C["sc"].sem, sc_cvs)
        last = None
        for ic in range(IC):
            last = ve.tensor_scalar(mstat[:, ic, :], onesb[:],
                                    cvs[:, ic:ic + 1], None, OP.mult)
        C["ve"].tag(last)
    sched.on("ve", _)
    C["ve"].n += 1
    ve_mstat = C["ve"].n

    wml_free = {}
    bank_free = {}
    mod_store = {}

    def emit_mod_og(og):
        buf = og % 2
        rel_te = wml_free.get(buf)

        def _(gp, og=og, buf=buf, rel_te=rel_te):
            if rel_te is not None:
                gp.wait_ge(C["te"].sem, rel_te)
            src = wmod_h[:, og * 512:(og + 1) * 512]
            src = src.rearrange("(ic p) c -> p ic c", p=128)
            i = gp.dma_start(out=wml[:, buf, :, :], in_=src)
            C["gpd"].tag(i, 16)
        sched.on("gp", _)
        C["gpd"].n += 16
        ld = C["gpd"].n
        rel_ve = bank_free.get(og - 4)

        def _(te, og=og, buf=buf, ld=ld, rel_ve=rel_ve):
            te.wait_ge(C["gpd"].sem, ld)
            if og == 0:
                te.wait_ge(C["ve"].sem, ve_mstat)
            if rel_ve is not None:
                te.wait_ge(C["ve"].sem, rel_ve)
            bank = og % 4
            ps = psA[:, bank * 512:(bank + 1) * 512]
            last = None
            for ic in range(IC):
                last = te.matmul(ps, mstat[:, ic, :], wml[:, buf, ic, :],
                                 start=(ic == 0), stop=(ic == IC - 1))
            C["te"].tag(last)
        sched.on("te", _)
        C["te"].n += 1
        wml_free[buf] = C["te"].n
        mm = C["te"].n
        vec, osl = og // 4, og % 4
        stg = gascr if og % 2 == 0 else gascr2
        prev_store = mod_store.get(og - 2)

        def _(ve, og=og, vec=vec, mm=mm, stg=stg, prev_store=prev_store):
            ve.wait_ge(C["te"].sem, mm)
            if prev_store is not None:
                ve.wait_ge(C["scd"].sem, prev_store)
            bank = og % 4
            ps = psA[:, bank * 512:(bank + 1) * 512]
            if vec in (0, 3):
                last = ve.tensor_copy(stg[:], ps)
            elif vec in (1, 4):
                last = ve.tensor_scalar(stg[:], ps, 1.0, None, OP.add)
            else:
                ve.tensor_single_scalar(stg[:], ps, 0.0, OP.is_ge)
                ve.tensor_scalar(stg[:], stg[:], 2.0, -1.0, OP.mult, OP.add)
                ve.tensor_scalar(t1[:], ps, -1.0, None, OP.mult)
                ve.tensor_max(t1[:], ps, t1[:])
                ve.tensor_scalar_max(t1[:], t1[:], 0.25)
                last = ve.tensor_tensor(stg[:], stg[:], t1[:], OP.mult)
            C["ve"].tag(last)
        sched.on("ve", _)
        C["ve"].n += 1
        bank_free[og] = C["ve"].n
        ev = C["ve"].n

        def _(sc, og=og, vec=vec, osl=osl, stg=stg, ev=ev):
            sc.wait_ge(C["ve"].sem, ev)
            dst = modsc_h[vec:vec + 1, osl * 512:(osl + 1) * 512]
            i = sc.dma_start(out=dst, in_=stg[0:1, :])
            C["scd"].tag(i, 16)
        sched.on("sc", _)
        C["scd"].n += 16
        mod_store[og] = C["scd"].n

    for og in range(8):
        emit_mod_og(og)
    saca_store = mod_store[7]

    def _(sy, saca_store=saca_store):
        sy.wait_ge(C["scd"].sem, saca_store)
        i = sy.dma_start(out=sa_t[:],
                         in_=modsc_h[0:1, :].partition_broadcast(128))
        C["sy"].tag(i, 16)
        i = sy.dma_start(out=ca1_t[:],
                         in_=modsc_h[1:2, :].partition_broadcast(128))
        C["sy"].tag(i, 16)
    sched.on("sy", _)
    C["sy"].n += 32
    saca_ld = C["sy"].n

    xin_rel = {}
    h1_rel = None
    ln1_tp_done = {}
    for tt in range(16):
        buf = tt % 2
        rel_sc = xin_rel.get(buf)

        def _(sy, tt=tt, buf=buf, rel_sc=rel_sc):
            if rel_sc is not None:
                sy.wait_ge(C["sc"].sem, rel_sc)
            i = sy.dma_start(out=xin2[:, buf, :],
                             in_=xkv_h[tt * 128:(tt + 1) * 128, :])
            C["sy"].tag(i, 16)
        sched.on("sy", _)
        C["sy"].n += 16
        ld = C["sy"].n

        def _(ve, buf=buf, ld=ld):
            ve.wait_ge(C["sy"].sem, ld)
            i = ve.reduce_sum(SCOL(0), xin2[:, buf, :], axis=AX.X)
            C["ve"].tag(i)
        sched.on("ve", _)
        C["ve"].n += 1
        ln1_red = C["ve"].n

        def _(sc, buf=buf, ld=ld, reltp=ln1_tp_done.get(tt - 1)):
            sc.wait_ge(C["sy"].sem, ld)
            if reltp is not None:
                sc.wait_ge(C["sy"].sem, reltp)
            i = sc.activation(sqout[:], xin2[:, buf, :], AF.Square,
                              accum_out=SCOL(1))
            C["sc"].tag(i)
        sched.on("sc", _)
        C["sc"].n += 1
        sc_sq = C["sc"].n

        def _(ve, sc_sq=sc_sq, ln1_red=ln1_red):
            ve.wait_ge(C["sc"].sem, sc_sq)
            ve.wait_ge(C["ve"].sem, ln1_red)
            ve.tensor_scalar(SB_(0), SB_(0), 1.0 / D, None, OP.mult)
            ve.tensor_scalar(SB_(1), SB_(1), 1.0 / D, None, OP.mult)
            ve.tensor_tensor(SB_(2), SB_(0), SB_(0), OP.mult)
            ve.tensor_sub(SB_(1), SB_(1), SB_(2))
            i = ve.tensor_scalar(SB_(1), SB_(1), EPS_LN, None, OP.add)
            C["ve"].tag(i)
        sched.on("ve", _)
        C["ve"].n += 1
        ve_var = C["ve"].n

        emit_newton(("ve", ve_var), SB_(4), SB_(1), SB_(5), SB_(6),
                    rsqrt=True)

        def _(ve):
            ve.tensor_tensor(SB_(3), SB_(0), SB_(4), OP.mult)
            i = ve.tensor_scalar(SB_(3), SB_(3), -1.0, None, OP.mult)
            C["ve"].tag(i)
        sched.on("ve", _)
        C["ve"].n += 1
        ve_rstd = C["ve"].n

        def _(sc, tt=tt, buf=buf, ve_rstd=ve_rstd):
            sc.wait_ge(C["ve"].sem, ve_rstd)
            if tt == 0:
                sc.wait_ge(C["sy"].sem, saca_ld)
            i = sc.activation(sqout[:], xin2[:, buf, :], AF.Identity,
                              bias=SCOL(3), scale=SCOL(4))
            C["sc"].tag(i)
        sched.on("sc", _)
        C["sc"].n += 1
        sc_xc = C["sc"].n
        xin_rel[buf] = sc_xc

        def _(ve, sc_xc=sc_xc, rel_sy=h1_rel):
            ve.wait_ge(C["sc"].sem, sc_xc)
            if rel_sy is not None:
                ve.wait_ge(C["sy"].sem, rel_sy)
            ve.tensor_tensor(sqout[:], sqout[:], ca1_t[:], OP.mult)
            i = ve.tensor_tensor(h1tok[:], sqout[:], sa_t[:], OP.add)
            C["ve"].tag(i)
        sched.on("ve", _)
        C["ve"].n += 1
        ve_h1 = C["ve"].n

        def _(sy, tt=tt, ve_h1=ve_h1):
            sy.wait_ge(C["ve"].sem, ve_h1)
            i = sy.dma_start_transpose(h1f[:, :, tt * 128:(tt + 1) * 128],
                                       h1tok[:])
            C["sy"].tag(i, 16)
        sched.on("sy", _)
        C["sy"].n += 16
        h1_rel = C["sy"].n
        ln1_tp_done[tt] = h1_rel

        emit_mod_og(8 + tt)

    barrier()

    if DEBUG_TAPS:
        def _(gp):
            i = gp.dma_start(out=dbg_h1[:], in_=h1f[:])
            C["out"].tag(i, 16)
            i = gp.dma_start(out=dbg_mod[:], in_=modsc_h[:])
            C["out"].tag(i, 16)
        sched.on("gp", _)
        C["out"].n += 32

    def subcat(delta, s):
        d = delta - s
        if d < 0 or d > 8:
            return "zero"
        if d == 0:
            return "diag"
        if d == 8:
            return "edge"
        return "band"

    qkv_w = [(wq_h, wqt), (wk_h, wkt), (wv_h, wvt)]
    lrow_stored = {}
    for hd in range(H):
        wt_ready = []
        for wi, (wh, wtile) in enumerate(qkv_w):
            sy_now = C["sy"].n

            def _(gp, wh=wh, hd=hd, sy_now=sy_now):
                gp.wait_ge(C["sy"].sem, sy_now)
                i = gp.dma_start(out=whload[:],
                                 in_=wh[hd * 128:(hd + 1) * 128, :])
                C["gpd"].tag(i, 16)
            sched.on("gp", _)
            C["gpd"].n += 16
            gld = C["gpd"].n

            def _(sy, wtile=wtile, gld=gld):
                sy.wait_ge(C["gpd"].sem, gld)
                i = sy.dma_start_transpose(wtile[:, :, :], whload[:])
                C["sy"].tag(i, 16)
            sched.on("sy", _)
            C["sy"].n += 16
            wt_ready.append(C["sy"].n)

        projs = [("k", wkt, ksb, 4, wt_ready[1]),
                 ("v", wvt, vsb, 4, wt_ready[2]),
                 ("q", wqt, qsb, 2, wt_ready[0])]
        prev_evac = None
        for pname, wtile, dsttile, ngr, wrdy in projs:
            def _(te, pname=pname, wtile=wtile, ngr=ngr, wrdy=wrdy,
                  prev_evac=prev_evac):
                te.wait_ge(C["sy"].sem, wrdy)
                if prev_evac is not None:
                    te.wait_ge(C["sc"].sem, prev_evac)
                last = None
                for ic in range(IC):
                    for g in range(ngr):
                        if pname == "q":
                            rhs = h1f[:, ic, 1024 + g * 512:1024 + (g + 1) * 512]
                        else:
                            rhs = h1f[:, ic, g * 512:(g + 1) * 512]
                        last = te.matmul(psB[:, g * 512:(g + 1) * 512],
                                         wtile[:, ic, :], rhs,
                                         start=(ic == 0), stop=(ic == IC - 1))
                C["te"].tag(last)
            sched.on("te", _)
            C["te"].n += 1
            mm_done = C["te"].n

            def _(sc, dsttile=dsttile, ngr=ngr, mm_done=mm_done):
                sc.wait_ge(C["te"].sem, mm_done)
                i = sc.activation(dsttile[:, 0:ngr * 512], psB[:, 0:ngr * 512],
                                  AF.Identity)
                C["sc"].tag(i)
            sched.on("sc", _)
            C["sc"].n += 1
            prev_evac = C["sc"].n
        qkv_evac = prev_evac

        def _(sy, qkv_evac=qkv_evac):
            sy.wait_ge(C["sc"].sem, qkv_evac)
            i = sy.dma_start_transpose(vt[:, :, :], vsb[:])
            C["sy"].tag(i, 16)
        sched.on("sy", _)
        C["sy"].n += 16
        vt_ready = C["sy"].n

        for g in range(2):
            pt_mask_done = []
            tanh_done = {}
            for jj in range(12):
                j = 4 * g + jj
                bank = jj % 2
                td2 = tanh_done.get(jj - 2)

                def _(te, j=j, jj=jj, g=g, bank=bank, qkv_evac=qkv_evac,
                      td2=td2):
                    if jj == 0 and g == 0:
                        te.wait_ge(C["sc"].sem, qkv_evac)
                    if td2 is not None:
                        te.wait_ge(C["sc"].sem, td2)
                    i = te.matmul(psA[:, bank * 512:(bank + 1) * 512],
                                  ksb[:, j * 128:(j + 1) * 128],
                                  qsb[:, g * 512:(g + 1) * 512],
                                  start=True, stop=True)
                    C["te"].tag(i)
                sched.on("te", _)
                C["te"].n += 1
                st_done = C["te"].n

                def _(sc, jj=jj, bank=bank, st_done=st_done):
                    sc.wait_ge(C["te"].sem, st_done)
                    sc.activation(tanhs[:, bank, :],
                                  psA[:, bank * 512:(bank + 1) * 512],
                                  AF.Tanh, scale=SCALE / SOFTCAP)
                    i = sc.activation(pt[:, jj, :], tanhs[:, bank, :],
                                      AF.Exp, scale=SOFTCAP)
                    C["sc"].tag(i)
                sched.on("sc", _)
                C["sc"].n += 1
                tanh_done[jj] = C["sc"].n
                exp_done = C["sc"].n

                def _(ve, j=j, jj=jj, exp_done=exp_done):
                    ve.wait_ge(C["sc"].sem, exp_done)
                    last = ve.tensor_tensor(pt[:, jj, :], pt[:, jj, :],
                                            mstr[:, jj, :], OP.mult)
                    if j <= 7:
                        last = ve.tensor_scalar(pt[:, jj, :], pt[:, jj, :],
                                                valid[:, j:j + 1], None,
                                                OP.mult)
                    C["ve"].tag(last)
                sched.on("ve", _)
                C["ve"].n += 1
                pt_mask_done.append(C["ve"].n)

            def _(te, g=g, vt_ready=vt_ready, pmd=list(pt_mask_done)):
                te.wait_ge(C["sy"].sem, vt_ready)
                last = None
                for jj in range(12):
                    j = 4 * g + jj
                    te.wait_ge(C["ve"].sem, pmd[jj])
                    te.matmul(psA[:, 1024:1536], onesb[:], pt[:, jj, :],
                              start=(jj == 0), stop=(jj == 11))
                    last = te.matmul(psA[:, 1536:2048], vt[:, j, :],
                                     pt[:, jj, :],
                                     start=(jj == 0), stop=(jj == 11))
                C["te"].tag(last)
            sched.on("te", _)
            C["te"].n += 1
            lav_done = C["te"].n

            def _(ve, hd=hd, g=g, lav_done=lav_done,
                  rel_st=lrow_stored.get(hd * 2 + g - 1)):
                ve.wait_ge(C["te"].sem, lav_done)
                if rel_st is not None:
                    ve.wait_ge(C["scd"].sem, rel_st)
                ve.tensor_copy(lnlb[:], psA[:, 1024:1536])
                i = ve.tensor_copy(attnsb[:, hd, g * 512:(g + 1) * 512],
                                   psA[:, 1536:2048])
                C["ve"].tag(i)
            sched.on("ve", _)
            C["ve"].n += 1
            ve_lrow = C["ve"].n

            def _(sc, hd=hd, g=g, ve_lrow=ve_lrow):
                sc.wait_ge(C["ve"].sem, ve_lrow)
                i = sc.dma_start(out=lsc_h[hd * 2 + g:hd * 2 + g + 1, :],
                                 in_=lnlb[0:1, :])
                C["scd"].tag(i, 16)
            sched.on("sc", _)
            C["scd"].n += 16
            lrow_stored[hd * 2 + g] = C["scd"].n

    barrier()

    if DEBUG_TAPS:
        def _(gp):
            i = gp.dma_start(out=dbg_at[:], in_=attnsb[:])
            C["out"].tag(i, 16)
        sched.on("gp", _)
        C["out"].n += 16

    def _(sy):
        i = sy.dma_start(out=ga_t[:],
                         in_=modsc_h[2:3, :].partition_broadcast(128))
        C["sy"].tag(i, 16)
        i = sy.dma_start(out=sb_t[:],
                         in_=modsc_h[3:4, :].partition_broadcast(128))
        C["sy"].tag(i, 16)
        i = sy.dma_start(out=cb1_t[:],
                         in_=modsc_h[4:5, :].partition_broadcast(128))
        C["sy"].tag(i, 16)
        for tt in range(8):
            i = sy.dma_start(out=xown[:, tt, :],
                             in_=xkv_h[1024 + tt * 128:1024 + (tt + 1) * 128, :])
            C["sy"].tag(i, 16)
    sched.on("sy", _)
    C["sy"].n += 176
    p2_ld = C["sy"].n

    norm_done = None
    lbc_free = {}
    for r in range(32):
        hd, g = r // 2, r % 2

        def _(sy, r=r, relv=lbc_free.get(r - 1)):
            if relv is not None:
                sy.wait_ge(C["ve"].sem, relv)
            i = sy.dma_start(out=lbc[:],
                             in_=lsc_h[r:r + 1, :].partition_broadcast(128))
            C["sy"].tag(i, 16)
        sched.on("sy", _)
        C["sy"].n += 16
        lld = C["sy"].n

        emit_newton(("sy", lld), rcn[:], lbc[:], gascr[:], gascr2[:])

        def _(ve, hd=hd, g=g):
            sl = attnsb[:, hd, g * 512:(g + 1) * 512]
            i = ve.tensor_tensor(sl, sl, rcn[:], OP.mult)
            C["ve"].tag(i)
        sched.on("ve", _)
        C["ve"].n += 1
        norm_done = C["ve"].n
        lbc_free[r] = norm_done

    ep_done = {}
    prev_og_mm = {}
    for og in range(4):
        wot_ready = None
        for ot in range(4):
            sy_now = C["sy"].n

            def _(gp, og=og, ot=ot, sy_now=sy_now,
                  relmm=prev_og_mm.get(og - 1)):
                gp.wait_ge(C["sy"].sem, sy_now)
                if ot == 0 and relmm is not None:
                    gp.wait_ge(C["te"].sem, relmm)
                r0 = og * 512 + ot * 128
                i = gp.dma_start(out=wotl[:], in_=wo_h[r0:r0 + 128, :])
                C["gpd"].tag(i, 16)
            sched.on("gp", _)
            C["gpd"].n += 16
            gld = C["gpd"].n

            def _(sy, ot=ot, gld=gld):
                sy.wait_ge(C["gpd"].sem, gld)
                i = sy.dma_start_transpose(wot[:, :, ot, :], wotl[:])
                C["sy"].tag(i, 16)
            sched.on("sy", _)
            C["sy"].n += 16
            wot_ready = C["sy"].n

        for tt in range(8):
            gi = og * 8 + tt
            bank = tt % 2
            rel_ve = ep_done.get(gi - 2)

            def _(te, og=og, tt=tt, bank=bank, wot_ready=wot_ready,
                  rel_ve=rel_ve, norm_done=norm_done):
                if tt == 0:
                    te.wait_ge(C["sy"].sem, wot_ready)
                    if og == 0:
                        te.wait_ge(C["sy"].sem, p2_ld)
                        te.wait_ge(C["ve"].sem, norm_done)
                if rel_ve is not None:
                    te.wait_ge(C["ve"].sem, rel_ve)
                last = None
                for cc in range(IC):
                    last = te.matmul(psA[:, bank * 512:(bank + 1) * 512],
                                     attnsb[:, cc, tt * 128:(tt + 1) * 128],
                                     wot[:, cc, :, :],
                                     start=(cc == 0), stop=(cc == IC - 1))
                C["te"].tag(last)
            sched.on("te", _)
            C["te"].n += 1
            mm_done = C["te"].n
            prev_og_mm[og] = mm_done

            def _(ve, og=og, tt=tt, bank=bank, mm_done=mm_done):
                ve.wait_ge(C["te"].sem, mm_done)
                ps = psA[:, bank * 512:(bank + 1) * 512]
                ve.tensor_tensor(t1[:], ps, ga_t[:, og * 512:(og + 1) * 512],
                                 OP.mult)
                i = ve.tensor_tensor(xown[:, tt, og * 512:(og + 1) * 512],
                                     xown[:, tt, og * 512:(og + 1) * 512],
                                     t1[:], OP.add)
                C["ve"].tag(i)
            sched.on("ve", _)
            C["ve"].n += 1
            ep_done[gi] = C["ve"].n
    oproj_ve = C["ve"].n

    xq_tp_done = {}
    xq_dq_done = {}
    for tt in range(8):
        def _(sc, tt=tt, oproj_ve=oproj_ve):
            if tt == 0:
                sc.wait_ge(C["ve"].sem, oproj_ve)
            i = sc.activation(hln2[:], xown[:, tt, :], AF.Square,
                              accum_out=SCOL(1))
            C["sc"].tag(i)
        sched.on("sc", _)
        C["sc"].n += 1
        sc_sq = C["sc"].n

        def _(sy, tt=tt, oproj_ve=oproj_ve):
            if tt == 0:
                sy.wait_ge(C["ve"].sem, oproj_ve)
            i = sy.dma_start(out=xmid_h[tt * 128:(tt + 1) * 128, :],
                             in_=xown[:, tt, :])
            C["sy"].tag(i, 16)
        sched.on("sy", _)
        C["sy"].n += 16

        def _(ve, tt=tt, sc_sq=sc_sq, nv0=C["ve"].n + 1):
            ve.wait_ge(C["sc"].sem, sc_sq)
            i = ve.reduce_sum(SCOL(0), xown[:, tt, :], axis=AX.X)
            C["ve"].tag(i)
            ve.wait_ge(C["ve"].sem, nv0)
            ve.tensor_scalar(SB_(0), SB_(0), 1.0 / D, None, OP.mult)
            ve.tensor_scalar(SB_(1), SB_(1), 1.0 / D, None, OP.mult)
            ve.tensor_tensor(SB_(2), SB_(0), SB_(0), OP.mult)
            ve.tensor_sub(SB_(1), SB_(1), SB_(2))
            i = ve.tensor_scalar(SB_(1), SB_(1), EPS_LN, None, OP.add)
            C["ve"].tag(i)
        sched.on("ve", _)
        C["ve"].n += 2
        ve_var = C["ve"].n

        emit_newton(("ve", ve_var), SB_(4), SB_(1), SB_(5), SB_(6),
                    rsqrt=True)

        def _(ve):
            ve.tensor_tensor(SB_(3), SB_(0), SB_(4), OP.mult)
            i = ve.tensor_scalar(SB_(3), SB_(3), -1.0, None, OP.mult)
            C["ve"].tag(i)
        sched.on("ve", _)
        C["ve"].n += 1
        ve_rstd = C["ve"].n

        def _(sc, tt=tt, ve_rstd=ve_rstd):
            sc.wait_ge(C["ve"].sem, ve_rstd)
            i = sc.activation(hln2[:], xown[:, tt, :], AF.Identity,
                              bias=SCOL(3), scale=SCOL(4))
            C["sc"].tag(i)
        sched.on("sc", _)
        C["sc"].n += 1
        sc_xc = C["sc"].n

        def _(ve, tt=tt, sc_xc=sc_xc, nv0=C["ve"].n + 1):
            ve.wait_ge(C["sc"].sem, sc_xc)
            ve.tensor_tensor(hln2[:], hln2[:], cb1_t[:], OP.mult)
            ve.tensor_tensor(hln2[:], hln2[:], sb_t[:], OP.add)
            i = ve.reduce_max(SCOL(7), hln2[:], axis=AX.X,
                              apply_absolute_value=True)
            C["ve"].tag(i)
            ve.wait_ge(C["ve"].sem, nv0)
            ve.tensor_scalar(SB_(7), SB_(7), 1.0 / QMAX, None, OP.mult)
            i = ve.tensor_scalar_max(SB_(7), SB_(7), EPS_Q)
            C["ve"].tag(i)
        sched.on("ve", _)
        C["ve"].n += 2
        ve_q0 = C["ve"].n
        nv = emit_newton(("ve", ve_q0), SB_(8), SB_(7), SB_(5), SB_(6))

        def _(gp, tt=tt, nv=nv):
            gp.wait_ge(C["ve"].sem, nv)
            gp.tensor_copy(sx[:, tt:tt + 1], SCOL(7))
            i = gp.tensor_copy(sxinv[:, tt:tt + 1], SCOL(8))
            C["gp"].tag(i)
        sched.on("gp", _)
        C["gp"].n += 1
        gp_cmp = C["gp"].n

        def _(sc, tt=tt, gp_cmp=gp_cmp, reldq=xq_dq_done.get(tt - 1)):
            sc.wait_ge(C["gp"].sem, gp_cmp)
            if reldq is not None:
                sc.wait_ge(C["gp"].sem, reldq)
            i = sc.activation(xqi8[:], hln2[:], AF.Identity,
                              scale=sxinv[:, tt:tt + 1])
            C["sc"].tag(i)
        sched.on("sc", _)
        C["sc"].n += 1
        sc_q = C["sc"].n

        def _(gp, sc_q=sc_q, reltp=xq_tp_done.get(tt - 1)):
            gp.wait_ge(C["sc"].sem, sc_q)
            if reltp is not None:
                gp.wait_ge(C["sy"].sem, reltp)
            i = gp.tensor_copy(xqb[:], xqi8[:])
            C["gp"].tag(i)
        sched.on("gp", _)
        C["gp"].n += 1
        gp_dq = C["gp"].n
        xq_dq_done[tt] = gp_dq

        def _(sy, tt=tt, gp_dq=gp_dq):
            sy.wait_ge(C["gp"].sem, gp_dq)
            i = sy.dma_start_transpose(xqf[:, :, tt * 128:(tt + 1) * 128],
                                       xqb[:])
            C["sy"].tag(i, 16)
        sched.on("sy", _)
        C["sy"].n += 16
        xq_tp_done[tt] = C["sy"].n

    barrier()

    if DEBUG_TAPS:
        def _(gp):
            i = gp.dma_start(out=dbg_xm[:], in_=xmid_h[:])
            C["out"].tag(i, 16)
            i = gp.dma_start(out=dbg_xqf[:], in_=xqf[:])
            C["out"].tag(i, 16)
        sched.on("gp", _)
        C["out"].n += 32

    gu_ep = {}
    p3_prev_cast = [0]
    for k in range(11):
        tdone = None
        for half, row0 in enumerate([512 * k, I_FFN + 512 * k]):
            wtile = w1tg if half == 0 else w1tu
            for q in range(2):
                r0 = row0 + q * 256
                sy_now = C["sy"].n

                def _(gp, r0=r0, sy_now=sy_now):
                    gp.wait_ge(C["sy"].sem, sy_now)
                    i = gp.dma_start(out=wload[:, 0, :],
                                     in_=wgu_h[r0:r0 + 128, :])
                    C["gpd"].tag(i, 16)
                    i = gp.dma_start(out=wload[:, 1, 0:512],
                                     in_=wgu_h[r0 + 128:r0 + 256, 0:512])
                    C["gpd"].tag(i, 16)
                sched.on("gp", _)
                C["gpd"].n += 32
                wld = C["gpd"].n

                def _(sy, r0=r0, pc=p3_prev_cast[0]):
                    if pc:
                        sy.wait_ge(C["ve"].sem, pc)
                    i = sy.dma_start(out=stage3[:],
                                     in_=wgu_h[r0 + 128:r0 + 256, 512:2048])
                    C["sy"].tag(i, 16)
                sched.on("sy", _)
                C["sy"].n += 16
                wld_s = C["sy"].n

                def _(ve, wld_s=wld_s, sy_now=sy_now):
                    ve.wait_ge(C["sy"].sem, wld_s)
                    ve.wait_ge(C["sy"].sem, sy_now)
                    i = ve.tensor_copy(wload[:, 1, 512:2048], stage3[:])
                    C["ve"].tag(i)
                sched.on("ve", _)
                C["ve"].n += 1
                p3_prev_cast[0] = C["ve"].n

                def _(ve, wld=wld, nv0=C["ve"].n):
                    ve.wait_ge(C["gpd"].sem, wld)
                    for rg in range(2):
                        i = ve.reduce_max(wsc[:, rg:rg + 1], wload[:, rg, :],
                                          axis=AX.X,
                                          apply_absolute_value=True)
                        C["ve"].tag(i)
                        ve.wait_ge(C["ve"].sem, nv0 + 1 + rg)
                    ve.tensor_scalar(wsc[:], wsc[:], 1.0 / QMAX, None,
                                     OP.mult)
                    i = ve.tensor_scalar_max(wsc[:], wsc[:], EPS_Q)
                    C["ve"].tag(i)
                sched.on("ve", _)
                C["ve"].n += 3
                ve_scan = C["ve"].n
                ve_sc = emit_newton(("ve", ve_scan), wsinv[:], wsc[:],
                                    gascr[:, 0:128], gascr2[:, 0:128])

                def _(sc, ve_sc=ve_sc):
                    sc.wait_ge(C["ve"].sem, ve_sc)
                    last = None
                    for rg in range(2):
                        last = sc.activation(wi8[:, rg, :], wload[:, rg, :],
                                             AF.Identity,
                                             scale=wsinv[:, rg:rg + 1])
                    C["sc"].tag(last)
                sched.on("sc", _)
                C["sc"].n += 1
                sc_q = C["sc"].n

                def _(gp, sc_q=sc_q):
                    gp.wait_ge(C["sc"].sem, sc_q)
                    last = None
                    for rg in range(2):
                        last = gp.tensor_scalar(wload[:, rg, :],
                                                wi8[:, rg, :],
                                                wsc[:, rg:rg + 1], None,
                                                OP.mult)
                    C["gp"].tag(last)
                sched.on("gp", _)
                C["gp"].n += 1
                gp_dq = C["gp"].n
                te_now = C["te"].n

                def _(sy, wtile=wtile, q=q, gp_dq=gp_dq, te_now=te_now):
                    sy.wait_ge(C["gp"].sem, gp_dq)
                    sy.wait_ge(C["te"].sem, te_now)
                    for rg in range(2):
                        c0 = (q * 2 + rg) * 128
                        i = sy.dma_start_transpose(
                            wtile[:, :, c0:c0 + 128], wload[:, rg, :])
                        C["sy"].tag(i, 16)
                sched.on("sy", _)
                C["sy"].n += 32
                tdone = C["sy"].n

        for tt in range(8):
            gi = k * 8 + tt
            bank = tt % 2
            rel_ve = gu_ep.get(gi - 2)

            def _(te, k=k, tt=tt, bank=bank, tdone=tdone, rel_ve=rel_ve):
                if tt == 0:
                    te.wait_ge(C["sy"].sem, tdone)
                if rel_ve is not None:
                    te.wait_ge(C["ve"].sem, rel_ve)
                psG = psA[:, bank * 512:(bank + 1) * 512]
                psU = psA[:, (2 + bank) * 512:(3 + bank) * 512]
                last = None
                for ic in range(IC):
                    te.matmul(psG, xqf[:, ic, tt * 128:(tt + 1) * 128],
                              w1tg[:, ic, :], start=(ic == 0),
                              stop=(ic == IC - 1))
                    last = te.matmul(psU, xqf[:, ic, tt * 128:(tt + 1) * 128],
                                     w1tu[:, ic, :], start=(ic == 0),
                                     stop=(ic == IC - 1))
                C["te"].tag(last)
            sched.on("te", _)
            C["te"].n += 1
            mm_done = C["te"].n

            def _(sc, tt=tt, bank=bank, mm_done=mm_done):
                sc.wait_ge(C["te"].sem, mm_done)
                i = sc.activation(sgt[:, tt % 2, :],
                                  psA[:, bank * 512:(bank + 1) * 512],
                                  AF.Silu, scale=sx[:, tt:tt + 1])
                C["sc"].tag(i)
            sched.on("sc", _)
            C["sc"].n += 1
            sc_silu = C["sc"].n

            def _(ve, k=k, tt=tt, bank=bank, sc_silu=sc_silu):
                ve.wait_ge(C["sc"].sem, sc_silu)
                i = ve.scalar_tensor_tensor(
                    arena[:, tt, k * 512:(k + 1) * 512],
                    psA[:, (2 + bank) * 512:(3 + bank) * 512],
                    sx[:, tt:tt + 1], sgt[:, tt % 2, :], OP.mult, OP.mult)
                C["ve"].tag(i)
            sched.on("ve", _)
            C["ve"].n += 1
            gu_ep[gi] = C["ve"].n

    barrier()

    def qslot(tt):
        return (tt + 8) % 9

    h2q_t_done = {}
    h2_dq_done = {}
    for tt in range(8):
        def _(ve, tt=tt, nv0=C["ve"].n + 1):
            i = ve.reduce_max(SCOL(7), arena[:, tt, :], axis=AX.X,
                              apply_absolute_value=True)
            C["ve"].tag(i)
            ve.wait_ge(C["ve"].sem, nv0)
            ve.tensor_scalar(SB_(7), SB_(7), 1.0 / QMAX, None, OP.mult)
            i = ve.tensor_scalar_max(SB_(7), SB_(7), EPS_Q)
            C["ve"].tag(i)
        sched.on("ve", _)
        C["ve"].n += 2
        ve_s0 = C["ve"].n
        nv = emit_newton(("ve", ve_s0), SB_(8), SB_(7), SB_(5), SB_(6))

        def _(gp, tt=tt, nv=nv):
            gp.wait_ge(C["ve"].sem, nv)
            gp.tensor_copy(sh2[:, tt:tt + 1], SCOL(7))
            i = gp.tensor_copy(sh2inv[:, tt:tt + 1], SCOL(8))
            C["gp"].tag(i)
        sched.on("gp", _)
        C["gp"].n += 1
        gp_cmp = C["gp"].n

        def _(sc, tt=tt, gp_cmp=gp_cmp, reldq=h2_dq_done.get(tt - 1)):
            sc.wait_ge(C["gp"].sem, gp_cmp)
            if reldq is not None:
                sc.wait_ge(C["gp"].sem, reldq)
            i = sc.activation(qscr8[:], arena[:, tt, :], AF.Identity,
                              scale=sh2inv[:, tt:tt + 1])
            C["sc"].tag(i)
        sched.on("sc", _)
        C["sc"].n += 1
        sc_q = C["sc"].n
        prev_t = h2q_t_done.get(tt - 1)

        def _(gp, sc_q=sc_q, prev_t=prev_t):
            gp.wait_ge(C["sc"].sem, sc_q)
            if prev_t is not None:
                gp.wait_ge(C["sy"].sem, prev_t)
            i = gp.tensor_copy(h2qb[:], qscr8[:])
            C["gp"].tag(i)
        sched.on("gp", _)
        C["gp"].n += 1
        gp_dq = C["gp"].n
        h2_dq_done[tt] = gp_dq

        def _(sy, tt=tt, gp_dq=gp_dq):
            sy.wait_ge(C["gp"].sem, gp_dq)
            dst = arena[:, qslot(tt), :].rearrange("p (ic c) -> p ic c", c=128)
            i = sy.dma_start_transpose(dst, h2qb[:])
            C["sy"].tag(i, 16)
        sched.on("sy", _)
        C["sy"].n += 16
        h2q_t_done[tt] = C["sy"].n
    h2q_all = C["sy"].n

    def _(sy):
        i = sy.dma_start(out=gb_t[:],
                         in_=modsc_h[5:6, :].partition_broadcast(128))
        C["sy"].tag(i, 16)
    sched.on("sy", _)
    C["sy"].n += 16
    gb_ld = C["sy"].n

    dn_ep = {}
    out_done = {}
    p4_prev_cast = [0]
    for ob in range(8):
        w2_ready = None
        for ot in range(2):
            blk = ob * 2 + ot
            sy_now = C["sy"].n

            def _(gp, blk=blk, sy_now=sy_now):
                gp.wait_ge(C["sy"].sem, sy_now)
                r0 = blk * 128
                i = gp.dma_start(out=w2load[:, 0:3840],
                                 in_=wd_h[r0:r0 + 128, 0:3840])
                C["gpd"].tag(i, 16)
            sched.on("gp", _)
            C["gpd"].n += 16
            wld = C["gpd"].n

            def _(sy, blk=blk, pc=p4_prev_cast[0]):
                if pc:
                    sy.wait_ge(C["ve"].sem, pc)
                r0 = blk * 128
                i = sy.dma_start(out=stage4[:],
                                 in_=wd_h[r0:r0 + 128, 3840:5632])
                C["sy"].tag(i, 16)
            sched.on("sy", _)
            C["sy"].n += 16
            wld_s = C["sy"].n

            def _(ve, wld_s=wld_s, sy_now=sy_now):
                ve.wait_ge(C["sy"].sem, wld_s)
                ve.wait_ge(C["sy"].sem, sy_now)
                i = ve.tensor_copy(w2load[:, 3840:5632], stage4[:])
                C["ve"].tag(i)
            sched.on("ve", _)
            C["ve"].n += 1
            p4_prev_cast[0] = C["ve"].n

            def _(ve, wld=wld, nv0=C["ve"].n + 1):
                ve.wait_ge(C["gpd"].sem, wld)
                i = ve.reduce_max(wsc[:, 0:1], w2load[:], axis=AX.X,
                                  apply_absolute_value=True)
                C["ve"].tag(i)
                ve.wait_ge(C["ve"].sem, nv0)
                ve.tensor_scalar(wsc[:], wsc[:], 1.0 / QMAX, None, OP.mult)
                i = ve.tensor_scalar_max(wsc[:], wsc[:], EPS_Q)
                C["ve"].tag(i)
            sched.on("ve", _)
            C["ve"].n += 2
            ve_scan = C["ve"].n
            ve_sc = emit_newton(("ve", ve_scan), wsinv[:], wsc[:],
                                gascr[:, 0:128], gascr2[:, 0:128])

            def _(sc, ve_sc=ve_sc):
                sc.wait_ge(C["ve"].sem, ve_sc)
                i = sc.activation(qscr8[:], w2load[:], AF.Identity,
                                  scale=wsinv[:, 0:1])
                C["sc"].tag(i)
            sched.on("sc", _)
            C["sc"].n += 1
            sc_q = C["sc"].n

            def _(gp, sc_q=sc_q):
                gp.wait_ge(C["sc"].sem, sc_q)
                i = gp.tensor_scalar(w2load[:], qscr8[:], wsc[:, 0:1], None,
                                     OP.mult)
                C["gp"].tag(i)
            sched.on("gp", _)
            C["gp"].n += 1
            gp_dq = C["gp"].n
            te_now = C["te"].n

            def _(sy, ot=ot, gp_dq=gp_dq, te_now=te_now):
                sy.wait_ge(C["gp"].sem, gp_dq)
                sy.wait_ge(C["te"].sem, te_now)
                i = sy.dma_start_transpose(w2t4[:, :, ot, :], w2load[:])
                C["sy"].tag(i, 16)
            sched.on("sy", _)
            C["sy"].n += 16
            w2_ready = C["sy"].n

        for tt in range(8):
            gi = ob * 8 + tt
            bank = tt % 2
            rel_ve = dn_ep.get(gi - 2)

            def _(sy, ob=ob, tt=tt, rel_ve=rel_ve):
                if rel_ve is not None:
                    sy.wait_ge(C["ve"].sem, rel_ve)
                src = xmid_h[tt * 128:(tt + 1) * 128,
                             ob * 256:(ob + 1) * 256]
                i = sy.dma_start(out=xmb[:, tt % 2, 0:256], in_=src)
                C["sy"].tag(i, 16)
            sched.on("sy", _)
            C["sy"].n += 16
            xld = C["sy"].n

            def _(te, ob=ob, tt=tt, bank=bank, w2_ready=w2_ready,
                  rel_ve=rel_ve):
                if tt == 0:
                    te.wait_ge(C["sy"].sem, w2_ready)
                    if ob == 0:
                        te.wait_ge(C["sy"].sem, h2q_all)
                if rel_ve is not None:
                    te.wait_ge(C["ve"].sem, rel_ve)
                ps = psA[:, bank * 512:bank * 512 + 256]
                last = None
                for ic in range(ICF):
                    last = te.matmul(
                        ps, arena[:, qslot(tt), ic * 128:(ic + 1) * 128],
                        w2t4[:, ic, :, :],
                        start=(ic == 0), stop=(ic == ICF - 1))
                C["te"].tag(last)
            sched.on("te", _)
            C["te"].n += 1
            mm_done = C["te"].n

            def _(ve, ob=ob, tt=tt, bank=bank, mm_done=mm_done, xld=xld,
                  gb_ld=gb_ld, relout=out_done.get(gi - 2)):
                ve.wait_ge(C["te"].sem, mm_done)
                ve.wait_ge(C["sy"].sem, max(xld, gb_ld))
                if relout is not None:
                    ve.wait_ge(C["out"].sem, relout)
                ps = psA[:, bank * 512:bank * 512 + 256]
                ve.tensor_tensor(t1[:, 0:256], ps,
                                 gb_t[:, ob * 256:(ob + 1) * 256], OP.mult)
                i = ve.scalar_tensor_tensor(
                    outscr[:, tt % 2, 0:256], t1[:, 0:256],
                    sh2[:, tt:tt + 1], xmb[:, tt % 2, 0:256],
                    OP.mult, OP.add)
                C["ve"].tag(i)
            sched.on("ve", _)
            C["ve"].n += 1
            dn_ep[gi] = C["ve"].n
            ep = C["ve"].n

            def _(sc, ob=ob, tt=tt, ep=ep):
                sc.wait_ge(C["ve"].sem, ep)
                dst = out_h[tt * 128:(tt + 1) * 128,
                            ob * 256:(ob + 1) * 256]
                i = sc.dma_start(out=dst, in_=outscr[:, tt % 2, 0:256])
                C["out"].tag(i, 16)
            sched.on("sc", _)
            C["out"].n += 16
            out_done[gi] = C["out"].n

    if DEBUG_TAPS:
        def _(gp):
            for col, src in ((0, sx), (8, sxinv), (16, sh2), (24, sh2inv)):
                i = gp.dma_start(out=dbg_s4[:, col:col + 8], in_=src[:])
                C["out"].tag(i, 16)
        sched.on("gp", _)
        C["out"].n += 64

    out_total = C["out"].n

    def _(gp):
        gp.wait_ge(C["out"].sem, out_total)
    sched.on("gp", _)

    with nc.Block() as block:
        @block.sync
        def _(eng):
            for f in sched.ops["sy"]:
                f(eng)

        @block.tensor
        def _(eng):
            for f in sched.ops["te"]:
                f(eng)

        @block.vector
        def _(eng):
            for f in sched.ops["ve"]:
                f(eng)

        @block.scalar
        def _(eng):
            for f in sched.ops["sc"]:
                f(eng)

        @block.gpsimd
        def _(eng):
            for f in sched.ops["gp"]:
                f(eng)

    for name, ch in C.items():
        assert ch.emitted == ch.n, (name, ch.emitted, ch.n)
    return nc


_NC_CACHE = {}


def kernel(L0_attn_x, L0_attn_w_q, L0_attn_w_k, L0_attn_w_v, L0_attn_w_o,
           L0_ffn_w_gate_up, L0_ffn_w_down, L0_c, L0_wmod):
    x = np.ascontiguousarray(np.asarray(L0_attn_x, dtype=np.float32))
    wq = np.ascontiguousarray(np.asarray(L0_attn_w_q, dtype=np.float32))
    wk = np.ascontiguousarray(np.asarray(L0_attn_w_k, dtype=np.float32))
    wv = np.ascontiguousarray(np.asarray(L0_attn_w_v, dtype=np.float32))
    wo = np.ascontiguousarray(np.asarray(L0_attn_w_o, dtype=np.float32))
    wgu = np.ascontiguousarray(np.asarray(L0_ffn_w_gate_up, dtype=np.float32))
    wd = np.ascontiguousarray(np.asarray(L0_ffn_w_down, dtype=np.float32))
    cvec = np.asarray(L0_c, dtype=np.float32)
    wmod = np.ascontiguousarray(np.asarray(L0_wmod, dtype=np.float32))

    kk = np.arange(128)[:, None]
    qq = np.arange(128)[None, :]
    mdiag = (kk > qq).astype(ml_dtypes.bfloat16)
    medge = (kk <= qq).astype(ml_dtypes.bfloat16)
    mstr = np.zeros((128, 12, 512), dtype=np.float32)
    for jj in range(12):
        for s in range(4):
            d = jj - s
            if d < 0 or d > 8:
                tile = 0.0
            elif d == 0:
                tile = (kk > qq).astype(np.float32)
            elif d == 8:
                tile = (kk <= qq).astype(np.float32)
            else:
                tile = 1.0
            mstr[:, jj, s * 128:(s + 1) * 128] = tile
    mstr = mstr.astype(ml_dtypes.bfloat16)

    if "nc" not in _NC_CACHE:
        _NC_CACHE["nc"] = build_nc()
    nc = _NC_CACHE["nc"]

    in_maps = []
    for c in range(8):
        b, hf = c // 2, c % 2
        t0 = 1024 * hf
        ctx = (np.zeros((1024, D), np.float32) if hf == 0 else x[b, 0:1024])
        xkv = np.ascontiguousarray(
            np.concatenate([ctx, x[b, t0:t0 + 1024]], axis=0))
        cv = np.ascontiguousarray(cvec[b].reshape(IC, 128).T)
        vld = np.ones((128, IC), np.float32)
        if hf == 0:
            vld[:, 0:8] = 0.0
        in_maps.append({
            "xkv": xkv,
            "wq": wq, "wk": wk, "wv": wv, "wo": wo,
            "wgu": wgu, "wd": wd, "wmod": wmod,
            "cv": cv, "valid": vld,
            "mdiag": mdiag, "medge": medge, "mstr": mstr,
        })

    res = run_bass_kernel_spmd(nc, in_maps, core_ids=list(range(8)))
    out = np.empty((4, 2048, D), np.float32)
    for c in range(8):
        b, hf = c // 2, c % 2
        out[b, 1024 * hf:1024 * hf + 1024] = res.results[c]["out"]
    return out


# version 04 (final)
# speedup vs baseline: 1.6471x; 1.6471x over previous
import numpy as np
import ml_dtypes
import concourse.bass as bass
import concourse.mybir as mybir
from concourse.bass_utils import run_bass_kernel_spmd

F32 = mybir.dt.float32
BF16 = mybir.dt.bfloat16
I8 = mybir.dt.int8
I16 = mybir.dt.int16
AF = mybir.ActivationFunctionType
OP = mybir.AluOpType
AX = mybir.AxisListType

D = 2048
IC = 16
H = 16
TOWN = 1024
TKV = 2048
I_FFN = 5632
ICF = 44
SCALE = 1.0 / np.sqrt(128.0)
SOFTCAP = 30.0
EPS_LN = 1e-5
EPS_Q = 1e-8
QMAX = 127.0
KB = 1024

ENGS = ("sy", "te", "ve", "sc", "gp")
DEBUG_TAPS = False


class Chan:
    def __init__(self, nc, name):
        self.ctx = nc.semaphore(name)
        self.sem = self.ctx.__enter__()
        self.n = 0
        self.emitted = 0

    def tag(self, instr, amt=1):
        instr.then_inc(self.sem, amt)
        self.emitted += amt


class Sched:
    def __init__(self):
        self.ops = {e: [] for e in ENGS}

    def on(self, eng, fn):
        self.ops[eng].append(fn)


class Alloc:

    def __init__(self, nc, total_bytes):
        self.nc = nc
        self.total = total_bytes
        self.backing_ctx = nc.sbuf_tensor("arena0", [128, total_bytes],
                                          mybir.dt.uint8)
        self.backing = self.backing_ctx.__enter__()
        self.base = nc.lookup_mloc(self.backing).addr
        self.live = []

    def t(self, name, shape, dtype, off, ph_lo, ph_hi):
        per_part = int(np.prod(shape[1:])) * mybir.dt.size(dtype)
        assert off % 32 == 0, name
        end = off + per_part
        assert end <= self.total, (name, end, self.total)
        for (n2, s2, e2, l2, h2) in self.live:
            if not (end <= s2 or e2 <= off or h2 < ph_lo or ph_hi < l2):
                raise AssertionError(f"sbuf overlap {name} vs {n2}")
        self.live.append((name, off, end, ph_lo, ph_hi))
        h = self.nc.virtual_sbuf_tensor(name, list(shape), dtype)
        self.nc._tensor(h.name, list(shape), dtype, type="SB")
        mloc = self.nc.lookup_mloc(h)
        mloc.allocated = True
        mloc.addr = self.base + off
        return h


def build_nc():
    nc = bass.Bass("TRN2", target_bir_lowering=False,
                   detect_race_conditions=False)

    xkv_h = nc.dram_tensor("xkv", [TKV, D], F32, kind="ExternalInput").ap()
    wq_h = nc.dram_tensor("wq", [D, D], F32, kind="ExternalInput").ap()
    wk_h = nc.dram_tensor("wk", [D, D], F32, kind="ExternalInput").ap()
    wv_h = nc.dram_tensor("wv", [D, D], F32, kind="ExternalInput").ap()
    wo_h = nc.dram_tensor("wo", [D, D], F32, kind="ExternalInput").ap()
    wgu_h = nc.dram_tensor("wgu", [2 * I_FFN, D], F32, kind="ExternalInput").ap()
    wd_h = nc.dram_tensor("wd", [D, I_FFN], F32, kind="ExternalInput").ap()
    wmod_h = nc.dram_tensor("wmod", [D, 6 * D], F32, kind="ExternalInput").ap()
    cv_h = nc.dram_tensor("cv", [128, IC], F32, kind="ExternalInput").ap()
    valid_h = nc.dram_tensor("valid", [128, IC], F32, kind="ExternalInput").ap()
    mdiag_h = nc.dram_tensor("mdiag", [128, 128], BF16, kind="ExternalInput").ap()
    medge_h = nc.dram_tensor("medge", [128, 128], BF16, kind="ExternalInput").ap()
    mstr_h = nc.dram_tensor("mstr", [128, 12, 512], BF16,
                            kind="ExternalInput").ap()
    out_h = nc.dram_tensor("out", [TOWN, D], F32, kind="ExternalOutput").ap()
    xmid_h = nc.dram_tensor("xmid", [TOWN, D], F32, kind="Internal").ap()
    modsc_h = nc.dram_tensor("modsc", [6, D], F32, kind="Internal").ap()
    lsc_h = nc.dram_tensor("lsc", [32, 512], F32, kind="Internal").ap()
    if DEBUG_TAPS:
        dbg_h1 = nc.dram_tensor("dbg_h1", [128, IC, TKV], BF16,
                                kind="ExternalOutput").ap()
        dbg_mod = nc.dram_tensor("dbg_mod", [6, D], F32,
                                 kind="ExternalOutput").ap()
        dbg_at = nc.dram_tensor("dbg_at", [128, H, TOWN], BF16,
                                kind="ExternalOutput").ap()
        dbg_xm = nc.dram_tensor("dbg_xm", [TOWN, D], F32,
                                kind="ExternalOutput").ap()
        dbg_xqf = nc.dram_tensor("dbg_xqf", [128, IC, TOWN], BF16,
                                 kind="ExternalOutput").ap()
        dbg_s4 = nc.dram_tensor("dbg_s4", [128, 32], F32,
                                kind="ExternalOutput").ap()

    al = Alloc(nc, 207 * KB)

    off0 = 0

    def sm(name, shape, dtype):
        nonlocal off0
        h = al.t(name, shape, dtype, off0, 0, 9)
        per_part = int(np.prod(shape[1:])) * mybir.dt.size(dtype)
        off0 += (per_part + 31) // 32 * 32
        return h

    onesb = sm("onesb", [128, 128], BF16)
    mdiag = sm("mdiag_s", [128, 128], BF16)
    medge = sm("medge_s", [128, 128], BF16)
    valid = sm("valid_s", [128, IC], F32)
    cvf = sm("cvf", [128, IC], F32)
    cvs = sm("cvs", [128, IC], F32)
    mstat = sm("mstat", [128, IC, 128], BF16)
    stat = sm("stat", [128, 16, 128], F32)
    kb1 = sm("kb1", [128, 128], F32)
    kb2 = sm("kb2", [128, 128], F32)
    sx = sm("sx", [128, 8], F32)
    sxinv = sm("sxinv", [128, 8], F32)
    sh2 = sm("sh2", [128, 8], F32)
    sh2inv = sm("sh2inv", [128, 8], F32)
    wsc = sm("wsc", [128, 128], F32)
    wsinv = sm("wsinv", [128, 128], F32)
    gascr = sm("gascr", [128, 512], F32)
    gascr2 = sm("gascr2", [128, 512], F32)
    assert off0 <= 20 * KB, off0

    t1 = al.t("t1", [128, 512], F32, 20 * KB, 0, 9)
    sgt = al.t("sgt", [128, 2, 512], BF16, 22 * KB, 3, 3)
    outscr = al.t("outscr", [128, 2, 512], F32, 24 * KB, 4, 4)
    xqb = al.t("xqb", [128, D], BF16, 28 * KB, 2, 2)
    wotl = al.t("wotl", [128, D], BF16, 28 * KB, 6, 6)
    xmb = al.t("xmb", [128, 2, 512], F32, 28 * KB, 4, 4)
    xqf = al.t("xqf", [128, IC, TOWN], BF16, 32 * KB, 2, 3)
    gb_t = al.t("gb", [128, D], F32, 32 * KB, 4, 4)
    qscr8 = al.t("qscr8", [128, I_FFN], I8, 40 * KB, 4, 4)
    tanhs = al.t("tanhs", [128, 2, 512], F32, 46 * KB, 1, 1)
    lnlb = al.t("lnlb", [128, 512], F32, 50 * KB, 1, 1)
    mstr = al.t("mstr", [128, 12, 512], BF16, 52 * KB, 0, 1)
    h2qb = al.t("h2qb", [128, I_FFN], BF16, 46 * KB, 4, 4)

    RA = 64 * KB
    h1f = al.t("h1f", [128, IC, TKV], BF16, RA, 0, 1)
    wml = al.t("wml", [128, 2, IC, 512], BF16, RA + 64 * KB, 0, 0)
    attnsb = al.t("attnsb", [128, H, TOWN], BF16, RA + 64 * KB, 1, 2)
    xown = al.t("xown", [128, 8, D], F32, RA, 2, 2)
    arena = al.t("arena", [128, 9, I_FFN], BF16, RA, 3, 4)

    RB = 163 * KB
    sa_t = al.t("sa", [128, D], F32, RB, 0, 0)
    ca1_t = al.t("ca1", [128, D], F32, RB + 8 * KB, 0, 0)
    xin2 = al.t("xin2", [128, 2, D], F32, RB + 16 * KB, 0, 0)
    sqout = al.t("sqout", [128, D], F32, RB + 32 * KB, 0, 0)
    h1tok = al.t("h1tok", [128, D], BF16, RB + 40 * KB, 0, 0)
    wqt = al.t("wqt", [128, IC, 128], BF16, RB, 1, 1)
    wkt = al.t("wkt", [128, IC, 128], BF16, RB + 4 * KB, 1, 1)
    wvt = al.t("wvt", [128, IC, 128], BF16, RB + 8 * KB, 1, 1)
    whload = al.t("whload", [128, D], BF16, RB + 12 * KB, 1, 1)
    ksb = al.t("ksb", [128, TKV], BF16, RB + 16 * KB, 1, 1)
    vsb = al.t("vsb", [128, TKV], BF16, RB + 20 * KB, 1, 1)
    vt = al.t("vt", [128, IC, 128], BF16, RB + 24 * KB, 1, 1)
    qsb = al.t("qsb", [128, TOWN], BF16, RB + 28 * KB, 1, 1)
    pt = al.t("pt", [128, 12, 512], BF16, RB + 30 * KB, 1, 1)
    ga_t = al.t("ga", [128, D], F32, RB, 2, 2)
    sb_t = al.t("sb", [128, D], F32, RB + 8 * KB, 2, 2)
    cb1_t = al.t("cb1", [128, D], F32, RB + 16 * KB, 2, 2)
    lbc = al.t("lbc", [128, 512], F32, RB + 24 * KB, 2, 2)
    rcn = al.t("rcn", [128, 512], F32, RB + 26 * KB, 2, 2)
    wot = al.t("wot", [128, IC, 4, 128], BF16, RB + 28 * KB, 2, 2)
    hln2 = al.t("hln2", [128, D], F32, RB + 28 * KB, 7, 7)
    xqi8 = al.t("xqi8", [128, D], I8, RB + 36 * KB, 7, 7)
    wload = al.t("wload", [128, 2, D], BF16, RB, 3, 3)
    wi8 = al.t("wi8", [128, 2, D], I8, RB + 8 * KB, 3, 3)
    w1tg = al.t("w1tg", [128, IC, 512], BF16, RB + 12 * KB, 3, 3)
    w1tu = al.t("w1tu", [128, IC, 512], BF16, RB + 28 * KB, 3, 3)
    stage3 = al.t("stage3", [128, 1536], F32, 26 * KB, 3, 3)
    w2t4 = al.t("w2t4", [128, ICF, 2, 128], BF16, RB, 4, 4)
    w2load = al.t("w2load", [128, I_FFN], BF16, RB + 22 * KB, 4, 4)
    stage4 = al.t("stage4", [128, 1792], F32, 57 * KB, 4, 4)

    psA_ctx = nc.psum_tensor("psA", [128, 2048], F32)
    psA = psA_ctx.__enter__()
    psB_ctx = nc.psum_tensor("psB", [128, 2048], F32)
    psB = psB_ctx.__enter__()

    sched = Sched()
    C = {n: Chan(nc, n) for n in
         ("sy", "gpd", "scd", "te", "ve", "sc", "gp", "out", "ph")}

    def emit_newton(dep, out, srcap, scr, tmp, rsqrt=False, iters=3):
        ch, val = dep

        def _(ve, ch=ch, val=val):
            ve.wait_ge(C[ch].sem, val)
            i = ve.memset(out, 0.0)
            C["ve"].tag(i)
        sched.on("ve", _)
        C["ve"].n += 1
        ve_z = C["ve"].n

        def _(sc, ch=ch, val=val, ve_z=ve_z):
            sc.wait_ge(C[ch].sem, val)
            sc.wait_ge(C["ve"].sem, ve_z)
            sc.activation(scr, srcap.bitcast(I16)[:, 1::2], AF.Identity)
            if rsqrt:
                sc.activation(scr, scr, AF.Identity, scale=-0.5,
                              bias=kb1[:, 0:1])
            else:
                sc.activation(scr, scr, AF.Identity, scale=-1.0,
                              bias=kb2[:, 0:1])
            i = sc.activation(out.bitcast(I16)[:, 1::2], scr, AF.Identity)
            C["sc"].tag(i)
        sched.on("sc", _)
        C["sc"].n += 1
        sc_seed = C["sc"].n

        def _(ve, sc_seed=sc_seed):
            ve.wait_ge(C["sc"].sem, sc_seed)
            i = None
            for _it in range(iters):
                if rsqrt:
                    ve.tensor_tensor(tmp, srcap, out, OP.mult)
                    ve.tensor_tensor(tmp, tmp, out, OP.mult)
                    ve.tensor_scalar(tmp, tmp, -0.5, 1.5, OP.mult, OP.add)
                    i = ve.tensor_tensor(out, out, tmp, OP.mult)
                else:
                    ve.tensor_tensor(tmp, srcap, out, OP.mult)
                    ve.tensor_scalar(tmp, tmp, -1.0, 2.0, OP.mult, OP.add)
                    i = ve.tensor_tensor(out, out, tmp, OP.mult)
            C["ve"].tag(i)
        sched.on("ve", _)
        C["ve"].n += 1
        return C["ve"].n

    def barrier():
        n_sy, n_gpd = C["sy"].n, C["gpd"].n
        n_scd, n_out = C["scd"].n, C["out"].n
        for e in ENGS:
            def mark(eng, e=e, n_sy=n_sy, n_gpd=n_gpd, n_scd=n_scd,
                     n_out=n_out):
                if e == "sy":
                    eng.wait_ge(C["sy"].sem, n_sy)
                    eng.sem_inc(C["ph"].sem, 1)
                    C["ph"].emitted += 1
                elif e == "gp":
                    eng.wait_ge(C["gpd"].sem, n_gpd)
                    C["ph"].tag(eng.nop(), 1)
                elif e == "sc":
                    eng.wait_ge(C["scd"].sem, n_scd)
                    eng.wait_ge(C["out"].sem, n_out)
                    C["ph"].tag(eng.nop(), 1)
                else:
                    C["ph"].tag(eng.nop(), 1)
            sched.on(e, mark)
        C["ph"].n += 5
        tgt = C["ph"].n
        for e in ENGS:
            sched.on(e, lambda eng, tgt=tgt: eng.wait_ge(C["ph"].sem, tgt))

    def SB_(b, w=128):
        return stat[:, b, 0:w]

    def SCOL(b):
        return stat[:, b, 0:1]

    def _(sy):
        for dst, src in ((mdiag, mdiag_h), (medge, medge_h),
                         (valid, valid_h), (cvf, cv_h), (mstr, mstr_h)):
            i = sy.dma_start(out=dst[:], in_=src[:])
            C["sy"].tag(i, 16)
    sched.on("sy", _)
    C["sy"].n += 80
    sy_small = C["sy"].n

    def _(ve):
        ve.memset(onesb[:], 1.0)
        ve.memset(kb1[:], 24375.0)
        ve.memset(kb2[:], 32499.0)
        ve.memset(stat[:], 1.0)
        i = ve.memset(wsc[:], 1.0)
        C["ve"].tag(i)
    sched.on("ve", _)
    C["ve"].n += 1

    def _(sc):
        sc.wait_ge(C["sy"].sem, sy_small)
        i = sc.activation(cvs[:], cvf[:], AF.Silu)
        C["sc"].tag(i)
    sched.on("sc", _)
    C["sc"].n += 1
    sc_cvs = C["sc"].n

    def _(ve):
        ve.wait_ge(C["sc"].sem, sc_cvs)
        last = None
        for ic in range(IC):
            last = ve.tensor_scalar(mstat[:, ic, :], onesb[:],
                                    cvs[:, ic:ic + 1], None, OP.mult)
        C["ve"].tag(last)
    sched.on("ve", _)
    C["ve"].n += 1
    ve_mstat = C["ve"].n

    wml_free = {}
    bank_free = {}
    mod_store = {}

    def emit_mod_og(og):
        buf = og % 2
        rel_te = wml_free.get(buf)

        def _(gp, og=og, buf=buf, rel_te=rel_te):
            if rel_te is not None:
                gp.wait_ge(C["te"].sem, rel_te)
            src = wmod_h[:, og * 512:(og + 1) * 512]
            src = src.rearrange("(ic p) c -> p ic c", p=128)
            i = gp.dma_start(out=wml[:, buf, :, :], in_=src)
            C["gpd"].tag(i, 16)
        sched.on("gp", _)
        C["gpd"].n += 16
        ld = C["gpd"].n
        rel_ve = bank_free.get(og - 4)

        def _(te, og=og, buf=buf, ld=ld, rel_ve=rel_ve):
            te.wait_ge(C["gpd"].sem, ld)
            if og == 0:
                te.wait_ge(C["ve"].sem, ve_mstat)
            if rel_ve is not None:
                te.wait_ge(C["ve"].sem, rel_ve)
            bank = og % 4
            ps = psA[:, bank * 512:(bank + 1) * 512]
            last = None
            for ic in range(IC):
                last = te.matmul(ps, mstat[:, ic, :], wml[:, buf, ic, :],
                                 start=(ic == 0), stop=(ic == IC - 1))
            C["te"].tag(last)
        sched.on("te", _)
        C["te"].n += 1
        wml_free[buf] = C["te"].n
        mm = C["te"].n
        vec, osl = og // 4, og % 4
        stg = gascr if og % 2 == 0 else gascr2
        prev_store = mod_store.get(og - 2)

        def _(ve, og=og, vec=vec, mm=mm, stg=stg, prev_store=prev_store):
            ve.wait_ge(C["te"].sem, mm)
            if prev_store is not None:
                ve.wait_ge(C["scd"].sem, prev_store)
            bank = og % 4
            ps = psA[:, bank * 512:(bank + 1) * 512]
            if vec in (0, 3):
                last = ve.tensor_copy(stg[:], ps)
            elif vec in (1, 4):
                last = ve.tensor_scalar(stg[:], ps, 1.0, None, OP.add)
            else:
                ve.tensor_single_scalar(stg[:], ps, 0.0, OP.is_ge)
                ve.tensor_scalar(stg[:], stg[:], 2.0, -1.0, OP.mult, OP.add)
                ve.tensor_scalar(t1[:], ps, -1.0, None, OP.mult)
                ve.tensor_max(t1[:], ps, t1[:])
                ve.tensor_scalar_max(t1[:], t1[:], 0.25)
                last = ve.tensor_tensor(stg[:], stg[:], t1[:], OP.mult)
            C["ve"].tag(last)
        sched.on("ve", _)
        C["ve"].n += 1
        bank_free[og] = C["ve"].n
        ev = C["ve"].n

        def _(sc, og=og, vec=vec, osl=osl, stg=stg, ev=ev):
            sc.wait_ge(C["ve"].sem, ev)
            dst = modsc_h[vec:vec + 1, osl * 512:(osl + 1) * 512]
            i = sc.dma_start(out=dst, in_=stg[0:1, :])
            C["scd"].tag(i, 16)
        sched.on("sc", _)
        C["scd"].n += 16
        mod_store[og] = C["scd"].n

    for og in range(8):
        emit_mod_og(og)
    saca_store = mod_store[7]

    def _(sy, saca_store=saca_store):
        sy.wait_ge(C["scd"].sem, saca_store)
        i = sy.dma_start(out=sa_t[:],
                         in_=modsc_h[0:1, :].partition_broadcast(128))
        C["sy"].tag(i, 16)
        i = sy.dma_start(out=ca1_t[:],
                         in_=modsc_h[1:2, :].partition_broadcast(128))
        C["sy"].tag(i, 16)
    sched.on("sy", _)
    C["sy"].n += 32
    saca_ld = C["sy"].n

    xin_rel = {}
    h1_rel = None
    ln1_tp_done = {}
    for tt in range(16):
        buf = tt % 2
        rel_sc = xin_rel.get(buf)

        def _(sy, tt=tt, buf=buf, rel_sc=rel_sc):
            if rel_sc is not None:
                sy.wait_ge(C["sc"].sem, rel_sc)
            i = sy.dma_start(out=xin2[:, buf, :],
                             in_=xkv_h[tt * 128:(tt + 1) * 128, :])
            C["sy"].tag(i, 16)
        sched.on("sy", _)
        C["sy"].n += 16
        ld = C["sy"].n

        def _(ve, buf=buf, ld=ld):
            ve.wait_ge(C["sy"].sem, ld)
            i = ve.reduce_sum(SCOL(0), xin2[:, buf, :], axis=AX.X)
            C["ve"].tag(i)
        sched.on("ve", _)
        C["ve"].n += 1
        ln1_red = C["ve"].n

        def _(sc, buf=buf, ld=ld, reltp=ln1_tp_done.get(tt - 1)):
            sc.wait_ge(C["sy"].sem, ld)
            if reltp is not None:
                sc.wait_ge(C["sy"].sem, reltp)
            i = sc.activation(sqout[:], xin2[:, buf, :], AF.Square,
                              accum_out=SCOL(1))
            C["sc"].tag(i)
        sched.on("sc", _)
        C["sc"].n += 1
        sc_sq = C["sc"].n

        def _(ve, sc_sq=sc_sq, ln1_red=ln1_red):
            ve.wait_ge(C["sc"].sem, sc_sq)
            ve.wait_ge(C["ve"].sem, ln1_red)
            ve.tensor_scalar(SB_(0), SB_(0), 1.0 / D, None, OP.mult)
            ve.tensor_scalar(SB_(1), SB_(1), 1.0 / D, None, OP.mult)
            ve.tensor_tensor(SB_(2), SB_(0), SB_(0), OP.mult)
            ve.tensor_sub(SB_(1), SB_(1), SB_(2))
            i = ve.tensor_scalar(SB_(1), SB_(1), EPS_LN, None, OP.add)
            C["ve"].tag(i)
        sched.on("ve", _)
        C["ve"].n += 1
        ve_var = C["ve"].n

        emit_newton(("ve", ve_var), SB_(4), SB_(1), SB_(5), SB_(6),
                    rsqrt=True)

        def _(ve):
            ve.tensor_tensor(SB_(3), SB_(0), SB_(4), OP.mult)
            i = ve.tensor_scalar(SB_(3), SB_(3), -1.0, None, OP.mult)
            C["ve"].tag(i)
        sched.on("ve", _)
        C["ve"].n += 1
        ve_rstd = C["ve"].n

        def _(sc, tt=tt, buf=buf, ve_rstd=ve_rstd):
            sc.wait_ge(C["ve"].sem, ve_rstd)
            if tt == 0:
                sc.wait_ge(C["sy"].sem, saca_ld)
            i = sc.activation(sqout[:], xin2[:, buf, :], AF.Identity,
                              bias=SCOL(3), scale=SCOL(4))
            C["sc"].tag(i)
        sched.on("sc", _)
        C["sc"].n += 1
        sc_xc = C["sc"].n
        xin_rel[buf] = sc_xc

        def _(ve, sc_xc=sc_xc, rel_sy=h1_rel):
            ve.wait_ge(C["sc"].sem, sc_xc)
            if rel_sy is not None:
                ve.wait_ge(C["sy"].sem, rel_sy)
            ve.tensor_tensor(sqout[:], sqout[:], ca1_t[:], OP.mult)
            i = ve.tensor_tensor(h1tok[:], sqout[:], sa_t[:], OP.add)
            C["ve"].tag(i)
        sched.on("ve", _)
        C["ve"].n += 1
        ve_h1 = C["ve"].n

        def _(sy, tt=tt, ve_h1=ve_h1):
            sy.wait_ge(C["ve"].sem, ve_h1)
            i = sy.dma_start_transpose(h1f[:, :, tt * 128:(tt + 1) * 128],
                                       h1tok[:])
            C["sy"].tag(i, 16)
        sched.on("sy", _)
        C["sy"].n += 16
        h1_rel = C["sy"].n
        ln1_tp_done[tt] = h1_rel

        emit_mod_og(8 + tt)

    barrier()

    if DEBUG_TAPS:
        def _(gp):
            i = gp.dma_start(out=dbg_h1[:], in_=h1f[:])
            C["out"].tag(i, 16)
            i = gp.dma_start(out=dbg_mod[:], in_=modsc_h[:])
            C["out"].tag(i, 16)
        sched.on("gp", _)
        C["out"].n += 32

    def subcat(delta, s):
        d = delta - s
        if d < 0 or d > 8:
            return "zero"
        if d == 0:
            return "diag"
        if d == 8:
            return "edge"
        return "band"

    qkv_w = [(wq_h, wqt), (wk_h, wkt), (wv_h, wvt)]
    lrow_stored = {}
    for hd in range(H):
        wt_ready = []
        for wi, (wh, wtile) in enumerate(qkv_w):
            sy_now = C["sy"].n

            def _(gp, wh=wh, hd=hd, sy_now=sy_now):
                gp.wait_ge(C["sy"].sem, sy_now)
                i = gp.dma_start(out=whload[:],
                                 in_=wh[hd * 128:(hd + 1) * 128, :])
                C["gpd"].tag(i, 16)
            sched.on("gp", _)
            C["gpd"].n += 16
            gld = C["gpd"].n

            def _(sy, wtile=wtile, gld=gld):
                sy.wait_ge(C["gpd"].sem, gld)
                i = sy.dma_start_transpose(wtile[:, :, :], whload[:])
                C["sy"].tag(i, 16)
            sched.on("sy", _)
            C["sy"].n += 16
            wt_ready.append(C["sy"].n)

        projs = [("k", wkt, ksb, 4, wt_ready[1]),
                 ("v", wvt, vsb, 4, wt_ready[2]),
                 ("q", wqt, qsb, 2, wt_ready[0])]
        prev_evac = None
        for pname, wtile, dsttile, ngr, wrdy in projs:
            def _(te, pname=pname, wtile=wtile, ngr=ngr, wrdy=wrdy,
                  prev_evac=prev_evac):
                te.wait_ge(C["sy"].sem, wrdy)
                if prev_evac is not None:
                    te.wait_ge(C["sc"].sem, prev_evac)
                last = None
                for ic in range(IC):
                    for g in range(ngr):
                        if pname == "q":
                            rhs = h1f[:, ic, 1024 + g * 512:1024 + (g + 1) * 512]
                        else:
                            rhs = h1f[:, ic, g * 512:(g + 1) * 512]
                        last = te.matmul(psB[:, g * 512:(g + 1) * 512],
                                         wtile[:, ic, :], rhs,
                                         start=(ic == 0), stop=(ic == IC - 1))
                C["te"].tag(last)
            sched.on("te", _)
            C["te"].n += 1
            mm_done = C["te"].n

            def _(sc, dsttile=dsttile, ngr=ngr, mm_done=mm_done):
                sc.wait_ge(C["te"].sem, mm_done)
                i = sc.activation(dsttile[:, 0:ngr * 512], psB[:, 0:ngr * 512],
                                  AF.Identity)
                C["sc"].tag(i)
            sched.on("sc", _)
            C["sc"].n += 1
            prev_evac = C["sc"].n
        qkv_evac = prev_evac

        def _(sy, qkv_evac=qkv_evac):
            sy.wait_ge(C["sc"].sem, qkv_evac)
            i = sy.dma_start_transpose(vt[:, :, :], vsb[:])
            C["sy"].tag(i, 16)
        sched.on("sy", _)
        C["sy"].n += 16
        vt_ready = C["sy"].n

        for g in range(2):
            pt_mask_done = []
            tanh_done = {}
            for jj in range(12):
                j = 4 * g + jj
                bank = jj % 2
                td2 = tanh_done.get(jj - 2)

                def _(te, j=j, jj=jj, g=g, bank=bank, qkv_evac=qkv_evac,
                      td2=td2):
                    if jj == 0 and g == 0:
                        te.wait_ge(C["sc"].sem, qkv_evac)
                    if td2 is not None:
                        te.wait_ge(C["sc"].sem, td2)
                    i = te.matmul(psA[:, bank * 512:(bank + 1) * 512],
                                  ksb[:, j * 128:(j + 1) * 128],
                                  qsb[:, g * 512:(g + 1) * 512],
                                  start=True, stop=True)
                    C["te"].tag(i)
                sched.on("te", _)
                C["te"].n += 1
                st_done = C["te"].n

                def _(sc, jj=jj, bank=bank, st_done=st_done):
                    sc.wait_ge(C["te"].sem, st_done)
                    sc.activation(tanhs[:, bank, :],
                                  psA[:, bank * 512:(bank + 1) * 512],
                                  AF.Tanh, scale=SCALE / SOFTCAP)
                    i = sc.activation(pt[:, jj, :], tanhs[:, bank, :],
                                      AF.Exp, scale=SOFTCAP)
                    C["sc"].tag(i)
                sched.on("sc", _)
                C["sc"].n += 1
                tanh_done[jj] = C["sc"].n
                exp_done = C["sc"].n

                def _(ve, j=j, jj=jj, exp_done=exp_done):
                    ve.wait_ge(C["sc"].sem, exp_done)
                    last = ve.tensor_tensor(pt[:, jj, :], pt[:, jj, :],
                                            mstr[:, jj, :], OP.mult)
                    if j <= 7:
                        last = ve.tensor_scalar(pt[:, jj, :], pt[:, jj, :],
                                                valid[:, j:j + 1], None,
                                                OP.mult)
                    C["ve"].tag(last)
                sched.on("ve", _)
                C["ve"].n += 1
                pt_mask_done.append(C["ve"].n)

            def _(te, g=g, vt_ready=vt_ready, pmd=list(pt_mask_done)):
                te.wait_ge(C["sy"].sem, vt_ready)
                last = None
                for jj in range(12):
                    j = 4 * g + jj
                    te.wait_ge(C["ve"].sem, pmd[jj])
                    te.matmul(psA[:, 1024:1536], onesb[:], pt[:, jj, :],
                              start=(jj == 0), stop=(jj == 11))
                    last = te.matmul(psA[:, 1536:2048], vt[:, j, :],
                                     pt[:, jj, :],
                                     start=(jj == 0), stop=(jj == 11))
                C["te"].tag(last)
            sched.on("te", _)
            C["te"].n += 1
            lav_done = C["te"].n

            def _(ve, hd=hd, g=g, lav_done=lav_done,
                  rel_st=lrow_stored.get(hd * 2 + g - 1)):
                ve.wait_ge(C["te"].sem, lav_done)
                if rel_st is not None:
                    ve.wait_ge(C["scd"].sem, rel_st)
                ve.tensor_copy(lnlb[:], psA[:, 1024:1536])
                i = ve.tensor_copy(attnsb[:, hd, g * 512:(g + 1) * 512],
                                   psA[:, 1536:2048])
                C["ve"].tag(i)
            sched.on("ve", _)
            C["ve"].n += 1
            ve_lrow = C["ve"].n

            def _(sc, hd=hd, g=g, ve_lrow=ve_lrow):
                sc.wait_ge(C["ve"].sem, ve_lrow)
                i = sc.dma_start(out=lsc_h[hd * 2 + g:hd * 2 + g + 1, :],
                                 in_=lnlb[0:1, :])
                C["scd"].tag(i, 16)
            sched.on("sc", _)
            C["scd"].n += 16
            lrow_stored[hd * 2 + g] = C["scd"].n

    barrier()

    if DEBUG_TAPS:
        def _(gp):
            i = gp.dma_start(out=dbg_at[:], in_=attnsb[:])
            C["out"].tag(i, 16)
        sched.on("gp", _)
        C["out"].n += 16

    def _(sy):
        i = sy.dma_start(out=ga_t[:],
                         in_=modsc_h[2:3, :].partition_broadcast(128))
        C["sy"].tag(i, 16)
        i = sy.dma_start(out=sb_t[:],
                         in_=modsc_h[3:4, :].partition_broadcast(128))
        C["sy"].tag(i, 16)
        i = sy.dma_start(out=cb1_t[:],
                         in_=modsc_h[4:5, :].partition_broadcast(128))
        C["sy"].tag(i, 16)
        for tt in range(8):
            i = sy.dma_start(out=xown[:, tt, :],
                             in_=xkv_h[1024 + tt * 128:1024 + (tt + 1) * 128, :])
            C["sy"].tag(i, 16)
    sched.on("sy", _)
    C["sy"].n += 176
    p2_ld = C["sy"].n

    norm_done = None
    lbc_free = {}
    for r in range(32):
        hd, g = r // 2, r % 2

        def _(sy, r=r, relv=lbc_free.get(r - 1)):
            if relv is not None:
                sy.wait_ge(C["ve"].sem, relv)
            i = sy.dma_start(out=lbc[:],
                             in_=lsc_h[r:r + 1, :].partition_broadcast(128))
            C["sy"].tag(i, 16)
        sched.on("sy", _)
        C["sy"].n += 16
        lld = C["sy"].n

        emit_newton(("sy", lld), rcn[:], lbc[:], gascr[:], gascr2[:])

        def _(ve, hd=hd, g=g):
            sl = attnsb[:, hd, g * 512:(g + 1) * 512]
            i = ve.tensor_tensor(sl, sl, rcn[:], OP.mult)
            C["ve"].tag(i)
        sched.on("ve", _)
        C["ve"].n += 1
        norm_done = C["ve"].n
        lbc_free[r] = norm_done

    ep_done = {}
    prev_og_mm = {}
    for og in range(4):
        wot_ready = None
        for ot in range(4):
            sy_now = C["sy"].n

            def _(gp, og=og, ot=ot, sy_now=sy_now,
                  relmm=prev_og_mm.get(og - 1)):
                gp.wait_ge(C["sy"].sem, sy_now)
                if ot == 0 and relmm is not None:
                    gp.wait_ge(C["te"].sem, relmm)
                r0 = og * 512 + ot * 128
                i = gp.dma_start(out=wotl[:], in_=wo_h[r0:r0 + 128, :])
                C["gpd"].tag(i, 16)
            sched.on("gp", _)
            C["gpd"].n += 16
            gld = C["gpd"].n

            def _(sy, ot=ot, gld=gld):
                sy.wait_ge(C["gpd"].sem, gld)
                i = sy.dma_start_transpose(wot[:, :, ot, :], wotl[:])
                C["sy"].tag(i, 16)
            sched.on("sy", _)
            C["sy"].n += 16
            wot_ready = C["sy"].n

        for tt in range(8):
            gi = og * 8 + tt
            bank = tt % 2
            rel_ve = ep_done.get(gi - 2)

            def _(te, og=og, tt=tt, bank=bank, wot_ready=wot_ready,
                  rel_ve=rel_ve, norm_done=norm_done):
                if tt == 0:
                    te.wait_ge(C["sy"].sem, wot_ready)
                    if og == 0:
                        te.wait_ge(C["sy"].sem, p2_ld)
                        te.wait_ge(C["ve"].sem, norm_done)
                if rel_ve is not None:
                    te.wait_ge(C["ve"].sem, rel_ve)
                last = None
                for cc in range(IC):
                    last = te.matmul(psA[:, bank * 512:(bank + 1) * 512],
                                     attnsb[:, cc, tt * 128:(tt + 1) * 128],
                                     wot[:, cc, :, :],
                                     start=(cc == 0), stop=(cc == IC - 1))
                C["te"].tag(last)
            sched.on("te", _)
            C["te"].n += 1
            mm_done = C["te"].n
            prev_og_mm[og] = mm_done

            def _(ve, og=og, tt=tt, bank=bank, mm_done=mm_done):
                ve.wait_ge(C["te"].sem, mm_done)
                ps = psA[:, bank * 512:(bank + 1) * 512]
                ve.tensor_tensor(t1[:], ps, ga_t[:, og * 512:(og + 1) * 512],
                                 OP.mult)
                i = ve.tensor_tensor(xown[:, tt, og * 512:(og + 1) * 512],
                                     xown[:, tt, og * 512:(og + 1) * 512],
                                     t1[:], OP.add)
                C["ve"].tag(i)
            sched.on("ve", _)
            C["ve"].n += 1
            ep_done[gi] = C["ve"].n
    oproj_ve = C["ve"].n

    xq_tp_done = {}
    xq_dq_done = {}
    for tt in range(8):
        def _(sc, tt=tt, oproj_ve=oproj_ve):
            if tt == 0:
                sc.wait_ge(C["ve"].sem, oproj_ve)
            i = sc.activation(hln2[:], xown[:, tt, :], AF.Square,
                              accum_out=SCOL(1))
            C["sc"].tag(i)
        sched.on("sc", _)
        C["sc"].n += 1
        sc_sq = C["sc"].n

        def _(sy, tt=tt, oproj_ve=oproj_ve):
            if tt == 0:
                sy.wait_ge(C["ve"].sem, oproj_ve)
            i = sy.dma_start(out=xmid_h[tt * 128:(tt + 1) * 128, :],
                             in_=xown[:, tt, :])
            C["sy"].tag(i, 16)
        sched.on("sy", _)
        C["sy"].n += 16

        def _(ve, tt=tt, sc_sq=sc_sq, nv0=C["ve"].n + 1):
            ve.wait_ge(C["sc"].sem, sc_sq)
            i = ve.reduce_sum(SCOL(0), xown[:, tt, :], axis=AX.X)
            C["ve"].tag(i)
            ve.wait_ge(C["ve"].sem, nv0)
            ve.tensor_scalar(SB_(0), SB_(0), 1.0 / D, None, OP.mult)
            ve.tensor_scalar(SB_(1), SB_(1), 1.0 / D, None, OP.mult)
            ve.tensor_tensor(SB_(2), SB_(0), SB_(0), OP.mult)
            ve.tensor_sub(SB_(1), SB_(1), SB_(2))
            i = ve.tensor_scalar(SB_(1), SB_(1), EPS_LN, None, OP.add)
            C["ve"].tag(i)
        sched.on("ve", _)
        C["ve"].n += 2
        ve_var = C["ve"].n

        emit_newton(("ve", ve_var), SB_(4), SB_(1), SB_(5), SB_(6),
                    rsqrt=True)

        def _(ve):
            ve.tensor_tensor(SB_(3), SB_(0), SB_(4), OP.mult)
            i = ve.tensor_scalar(SB_(3), SB_(3), -1.0, None, OP.mult)
            C["ve"].tag(i)
        sched.on("ve", _)
        C["ve"].n += 1
        ve_rstd = C["ve"].n

        def _(sc, tt=tt, ve_rstd=ve_rstd):
            sc.wait_ge(C["ve"].sem, ve_rstd)
            i = sc.activation(hln2[:], xown[:, tt, :], AF.Identity,
                              bias=SCOL(3), scale=SCOL(4))
            C["sc"].tag(i)
        sched.on("sc", _)
        C["sc"].n += 1
        sc_xc = C["sc"].n

        def _(ve, tt=tt, sc_xc=sc_xc, nv0=C["ve"].n + 1):
            ve.wait_ge(C["sc"].sem, sc_xc)
            ve.tensor_tensor(hln2[:], hln2[:], cb1_t[:], OP.mult)
            ve.tensor_tensor(hln2[:], hln2[:], sb_t[:], OP.add)
            i = ve.reduce_max(SCOL(7), hln2[:], axis=AX.X,
                              apply_absolute_value=True)
            C["ve"].tag(i)
            ve.wait_ge(C["ve"].sem, nv0)
            ve.tensor_scalar(SB_(7), SB_(7), 1.0 / QMAX, None, OP.mult)
            i = ve.tensor_scalar_max(SB_(7), SB_(7), EPS_Q)
            C["ve"].tag(i)
        sched.on("ve", _)
        C["ve"].n += 2
        ve_q0 = C["ve"].n
        nv = emit_newton(("ve", ve_q0), SB_(8), SB_(7), SB_(5), SB_(6))

        def _(sc, tt=tt, nv=nv, reltp=xq_tp_done.get(tt - 1)):
            sc.wait_ge(C["ve"].sem, nv)
            sc.activation(sx[:, tt:tt + 1], SCOL(7), AF.Identity)
            sc.activation(sxinv[:, tt:tt + 1], SCOL(8), AF.Identity)
            sc.activation(xqi8[:], hln2[:], AF.Identity,
                          scale=sxinv[:, tt:tt + 1])
            if reltp is not None:
                sc.wait_ge(C["sy"].sem, reltp)
            i = sc.activation(xqb[:], xqi8[:], AF.Identity)
            C["sc"].tag(i)
        sched.on("sc", _)
        C["sc"].n += 1
        gp_dq = C["sc"].n
        xq_dq_done[tt] = gp_dq

        def _(sy, tt=tt, gp_dq=gp_dq):
            sy.wait_ge(C["sc"].sem, gp_dq)
            i = sy.dma_start_transpose(xqf[:, :, tt * 128:(tt + 1) * 128],
                                       xqb[:])
            C["sy"].tag(i, 16)
        sched.on("sy", _)
        C["sy"].n += 16
        xq_tp_done[tt] = C["sy"].n

    barrier()

    if DEBUG_TAPS:
        def _(gp):
            i = gp.dma_start(out=dbg_xm[:], in_=xmid_h[:])
            C["out"].tag(i, 16)
            i = gp.dma_start(out=dbg_xqf[:], in_=xqf[:])
            C["out"].tag(i, 16)
        sched.on("gp", _)
        C["out"].n += 32

    gu_ep = {}
    p3_prev_cast = [0]
    for k in range(11):
        tdone = None
        for half, row0 in enumerate([512 * k, I_FFN + 512 * k]):
            wtile = w1tg if half == 0 else w1tu
            for q in range(2):
                r0 = row0 + q * 256
                sy_now = C["sy"].n

                def _(gp, r0=r0, sy_now=sy_now):
                    gp.wait_ge(C["sy"].sem, sy_now)
                    i = gp.dma_start(out=wload[:, 0, :],
                                     in_=wgu_h[r0:r0 + 128, :])
                    C["gpd"].tag(i, 16)
                    i = gp.dma_start(out=wload[:, 1, 0:512],
                                     in_=wgu_h[r0 + 128:r0 + 256, 0:512])
                    C["gpd"].tag(i, 16)
                sched.on("gp", _)
                C["gpd"].n += 32
                wld = C["gpd"].n

                def _(sy, r0=r0, pc=p3_prev_cast[0]):
                    if pc:
                        sy.wait_ge(C["ve"].sem, pc)
                    i = sy.dma_start(out=stage3[:],
                                     in_=wgu_h[r0 + 128:r0 + 256, 512:2048])
                    C["sy"].tag(i, 16)
                sched.on("sy", _)
                C["sy"].n += 16
                wld_s = C["sy"].n

                def _(ve, wld_s=wld_s, sy_now=sy_now):
                    ve.wait_ge(C["sy"].sem, wld_s)
                    ve.wait_ge(C["sy"].sem, sy_now)
                    i = ve.tensor_copy(wload[:, 1, 512:2048], stage3[:])
                    C["ve"].tag(i)
                sched.on("ve", _)
                C["ve"].n += 1
                p3_prev_cast[0] = C["ve"].n

                def _(ve, wld=wld, nv0=C["ve"].n):
                    ve.wait_ge(C["gpd"].sem, wld)
                    for rg in range(2):
                        i = ve.reduce_max(wsc[:, rg:rg + 1], wload[:, rg, :],
                                          axis=AX.X,
                                          apply_absolute_value=True)
                        C["ve"].tag(i)
                        ve.wait_ge(C["ve"].sem, nv0 + 1 + rg)
                    ve.tensor_scalar(wsc[:], wsc[:], 1.0 / QMAX, None,
                                     OP.mult)
                    i = ve.tensor_scalar_max(wsc[:], wsc[:], EPS_Q)
                    C["ve"].tag(i)
                sched.on("ve", _)
                C["ve"].n += 3
                ve_scan = C["ve"].n
                ve_sc = emit_newton(("ve", ve_scan), wsinv[:], wsc[:],
                                    gascr[:, 0:128], gascr2[:, 0:128])

                def _(sc, ve_sc=ve_sc):
                    sc.wait_ge(C["ve"].sem, ve_sc)
                    for rg in range(2):
                        sc.activation(wi8[:, rg, :], wload[:, rg, :],
                                      AF.Identity,
                                      scale=wsinv[:, rg:rg + 1])
                    last = None
                    for rg in range(2):
                        last = sc.activation(wload[:, rg, :], wi8[:, rg, :],
                                             AF.Identity,
                                             scale=wsc[:, rg:rg + 1])
                    C["sc"].tag(last)
                sched.on("sc", _)
                C["sc"].n += 1
                gp_dq = C["sc"].n
                te_now = C["te"].n

                def _(sy, wtile=wtile, q=q, gp_dq=gp_dq, te_now=te_now):
                    sy.wait_ge(C["sc"].sem, gp_dq)
                    sy.wait_ge(C["te"].sem, te_now)
                    for rg in range(2):
                        c0 = (q * 2 + rg) * 128
                        i = sy.dma_start_transpose(
                            wtile[:, :, c0:c0 + 128], wload[:, rg, :])
                        C["sy"].tag(i, 16)
                sched.on("sy", _)
                C["sy"].n += 32
                tdone = C["sy"].n

        for tt in range(8):
            gi = k * 8 + tt
            bank = tt % 2
            rel_ve = gu_ep.get(gi - 2)

            def _(te, k=k, tt=tt, bank=bank, tdone=tdone, rel_ve=rel_ve):
                if tt == 0:
                    te.wait_ge(C["sy"].sem, tdone)
                if rel_ve is not None:
                    te.wait_ge(C["ve"].sem, rel_ve)
                psG = psA[:, bank * 512:(bank + 1) * 512]
                psU = psA[:, (2 + bank) * 512:(3 + bank) * 512]
                last = None
                for ic in range(IC):
                    te.matmul(psG, xqf[:, ic, tt * 128:(tt + 1) * 128],
                              w1tg[:, ic, :], start=(ic == 0),
                              stop=(ic == IC - 1))
                    last = te.matmul(psU, xqf[:, ic, tt * 128:(tt + 1) * 128],
                                     w1tu[:, ic, :], start=(ic == 0),
                                     stop=(ic == IC - 1))
                C["te"].tag(last)
            sched.on("te", _)
            C["te"].n += 1
            mm_done = C["te"].n

            def _(sc, tt=tt, bank=bank, mm_done=mm_done):
                sc.wait_ge(C["te"].sem, mm_done)
                i = sc.activation(sgt[:, tt % 2, :],
                                  psA[:, bank * 512:(bank + 1) * 512],
                                  AF.Silu, scale=sx[:, tt:tt + 1])
                C["sc"].tag(i)
            sched.on("sc", _)
            C["sc"].n += 1
            sc_silu = C["sc"].n

            def _(ve, k=k, tt=tt, bank=bank, sc_silu=sc_silu):
                ve.wait_ge(C["sc"].sem, sc_silu)
                i = ve.scalar_tensor_tensor(
                    arena[:, tt, k * 512:(k + 1) * 512],
                    psA[:, (2 + bank) * 512:(3 + bank) * 512],
                    sx[:, tt:tt + 1], sgt[:, tt % 2, :], OP.mult, OP.mult)
                C["ve"].tag(i)
            sched.on("ve", _)
            C["ve"].n += 1
            gu_ep[gi] = C["ve"].n

    barrier()

    def qslot(tt):
        return (tt + 8) % 9

    h2q_t_done = {}
    h2_dq_done = {}
    for tt in range(8):
        def _(ve, tt=tt, nv0=C["ve"].n + 1):
            i = ve.reduce_max(SCOL(7), arena[:, tt, :], axis=AX.X,
                              apply_absolute_value=True)
            C["ve"].tag(i)
            ve.wait_ge(C["ve"].sem, nv0)
            ve.tensor_scalar(SB_(7), SB_(7), 1.0 / QMAX, None, OP.mult)
            i = ve.tensor_scalar_max(SB_(7), SB_(7), EPS_Q)
            C["ve"].tag(i)
        sched.on("ve", _)
        C["ve"].n += 2
        ve_s0 = C["ve"].n
        nv = emit_newton(("ve", ve_s0), SB_(8), SB_(7), SB_(5), SB_(6))

        def _(sc, tt=tt, nv=nv, prev_t=h2q_t_done.get(tt - 1)):
            sc.wait_ge(C["ve"].sem, nv)
            sc.activation(sh2[:, tt:tt + 1], SCOL(7), AF.Identity)
            sc.activation(sh2inv[:, tt:tt + 1], SCOL(8), AF.Identity)
            sc.activation(qscr8[:], arena[:, tt, :], AF.Identity,
                          scale=sh2inv[:, tt:tt + 1])
            if prev_t is not None:
                sc.wait_ge(C["sy"].sem, prev_t)
            i = sc.activation(h2qb[:], qscr8[:], AF.Identity)
            C["sc"].tag(i)
        sched.on("sc", _)
        C["sc"].n += 1
        gp_dq = C["sc"].n
        h2_dq_done[tt] = gp_dq

        def _(sy, tt=tt, gp_dq=gp_dq):
            sy.wait_ge(C["sc"].sem, gp_dq)
            dst = arena[:, qslot(tt), :].rearrange("p (ic c) -> p ic c", c=128)
            i = sy.dma_start_transpose(dst, h2qb[:])
            C["sy"].tag(i, 16)
        sched.on("sy", _)
        C["sy"].n += 16
        h2q_t_done[tt] = C["sy"].n
    h2q_all = C["sy"].n

    def _(sy):
        i = sy.dma_start(out=gb_t[:],
                         in_=modsc_h[5:6, :].partition_broadcast(128))
        C["sy"].tag(i, 16)
    sched.on("sy", _)
    C["sy"].n += 16
    gb_ld = C["sy"].n

    dn_ep = {}
    out_done = {}
    p4_prev_cast = [0]
    for ob in range(8):
        w2_ready = None
        for ot in range(2):
            blk = ob * 2 + ot
            sy_now = C["sy"].n

            def _(gp, blk=blk, sy_now=sy_now):
                gp.wait_ge(C["sy"].sem, sy_now)
                r0 = blk * 128
                i = gp.dma_start(out=w2load[:, 0:3840],
                                 in_=wd_h[r0:r0 + 128, 0:3840])
                C["gpd"].tag(i, 16)
            sched.on("gp", _)
            C["gpd"].n += 16
            wld = C["gpd"].n

            def _(sy, blk=blk, pc=p4_prev_cast[0]):
                if pc:
                    sy.wait_ge(C["ve"].sem, pc)
                r0 = blk * 128
                i = sy.dma_start(out=stage4[:],
                                 in_=wd_h[r0:r0 + 128, 3840:5632])
                C["sy"].tag(i, 16)
            sched.on("sy", _)
            C["sy"].n += 16
            wld_s = C["sy"].n

            def _(ve, wld_s=wld_s, sy_now=sy_now):
                ve.wait_ge(C["sy"].sem, wld_s)
                ve.wait_ge(C["sy"].sem, sy_now)
                i = ve.tensor_copy(w2load[:, 3840:5632], stage4[:])
                C["ve"].tag(i)
            sched.on("ve", _)
            C["ve"].n += 1
            p4_prev_cast[0] = C["ve"].n

            def _(ve, wld=wld, nv0=C["ve"].n + 1):
                ve.wait_ge(C["gpd"].sem, wld)
                i = ve.reduce_max(wsc[:, 0:1], w2load[:], axis=AX.X,
                                  apply_absolute_value=True)
                C["ve"].tag(i)
                ve.wait_ge(C["ve"].sem, nv0)
                ve.tensor_scalar(wsc[:], wsc[:], 1.0 / QMAX, None, OP.mult)
                i = ve.tensor_scalar_max(wsc[:], wsc[:], EPS_Q)
                C["ve"].tag(i)
            sched.on("ve", _)
            C["ve"].n += 2
            ve_scan = C["ve"].n
            ve_sc = emit_newton(("ve", ve_scan), wsinv[:], wsc[:],
                                gascr[:, 0:128], gascr2[:, 0:128])

            def _(sc, ve_sc=ve_sc):
                sc.wait_ge(C["ve"].sem, ve_sc)
                sc.activation(qscr8[:], w2load[:], AF.Identity,
                              scale=wsinv[:, 0:1])
                i = sc.activation(w2load[:], qscr8[:], AF.Identity,
                                  scale=wsc[:, 0:1])
                C["sc"].tag(i)
            sched.on("sc", _)
            C["sc"].n += 1
            gp_dq = C["sc"].n
            te_now = C["te"].n

            def _(sy, ot=ot, gp_dq=gp_dq, te_now=te_now):
                sy.wait_ge(C["sc"].sem, gp_dq)
                sy.wait_ge(C["te"].sem, te_now)
                i = sy.dma_start_transpose(w2t4[:, :, ot, :], w2load[:])
                C["sy"].tag(i, 16)
            sched.on("sy", _)
            C["sy"].n += 16
            w2_ready = C["sy"].n

        for tt in range(8):
            gi = ob * 8 + tt
            bank = tt % 2
            rel_ve = dn_ep.get(gi - 2)

            def _(sy, ob=ob, tt=tt, rel_ve=rel_ve):
                if rel_ve is not None:
                    sy.wait_ge(C["ve"].sem, rel_ve)
                src = xmid_h[tt * 128:(tt + 1) * 128,
                             ob * 256:(ob + 1) * 256]
                i = sy.dma_start(out=xmb[:, tt % 2, 0:256], in_=src)
                C["sy"].tag(i, 16)
            sched.on("sy", _)
            C["sy"].n += 16
            xld = C["sy"].n

            def _(te, ob=ob, tt=tt, bank=bank, w2_ready=w2_ready,
                  rel_ve=rel_ve):
                if tt == 0:
                    te.wait_ge(C["sy"].sem, w2_ready)
                    if ob == 0:
                        te.wait_ge(C["sy"].sem, h2q_all)
                if rel_ve is not None:
                    te.wait_ge(C["ve"].sem, rel_ve)
                ps = psA[:, bank * 512:bank * 512 + 256]
                last = None
                for ic in range(ICF):
                    last = te.matmul(
                        ps, arena[:, qslot(tt), ic * 128:(ic + 1) * 128],
                        w2t4[:, ic, :, :],
                        start=(ic == 0), stop=(ic == ICF - 1))
                C["te"].tag(last)
            sched.on("te", _)
            C["te"].n += 1
            mm_done = C["te"].n

            def _(ve, ob=ob, tt=tt, bank=bank, mm_done=mm_done, xld=xld,
                  gb_ld=gb_ld, relout=out_done.get(gi - 2)):
                ve.wait_ge(C["te"].sem, mm_done)
                ve.wait_ge(C["sy"].sem, max(xld, gb_ld))
                if relout is not None:
                    ve.wait_ge(C["out"].sem, relout)
                ps = psA[:, bank * 512:bank * 512 + 256]
                ve.tensor_tensor(t1[:, 0:256], ps,
                                 gb_t[:, ob * 256:(ob + 1) * 256], OP.mult)
                i = ve.scalar_tensor_tensor(
                    outscr[:, tt % 2, 0:256], t1[:, 0:256],
                    sh2[:, tt:tt + 1], xmb[:, tt % 2, 0:256],
                    OP.mult, OP.add)
                C["ve"].tag(i)
            sched.on("ve", _)
            C["ve"].n += 1
            dn_ep[gi] = C["ve"].n
            ep = C["ve"].n

            def _(sc, ob=ob, tt=tt, ep=ep):
                sc.wait_ge(C["ve"].sem, ep)
                dst = out_h[tt * 128:(tt + 1) * 128,
                            ob * 256:(ob + 1) * 256]
                i = sc.dma_start(out=dst, in_=outscr[:, tt % 2, 0:256])
                C["out"].tag(i, 16)
            sched.on("sc", _)
            C["out"].n += 16
            out_done[gi] = C["out"].n

    if DEBUG_TAPS:
        def _(gp):
            for col, src in ((0, sx), (8, sxinv), (16, sh2), (24, sh2inv)):
                i = gp.dma_start(out=dbg_s4[:, col:col + 8], in_=src[:])
                C["out"].tag(i, 16)
        sched.on("gp", _)
        C["out"].n += 64

    out_total = C["out"].n

    def _(gp):
        gp.wait_ge(C["out"].sem, out_total)
    sched.on("gp", _)

    with nc.Block() as block:
        @block.sync
        def _(eng):
            for f in sched.ops["sy"]:
                f(eng)

        @block.tensor
        def _(eng):
            for f in sched.ops["te"]:
                f(eng)

        @block.vector
        def _(eng):
            for f in sched.ops["ve"]:
                f(eng)

        @block.scalar
        def _(eng):
            for f in sched.ops["sc"]:
                f(eng)

        @block.gpsimd
        def _(eng):
            for f in sched.ops["gp"]:
                f(eng)

    for name, ch in C.items():
        assert ch.emitted == ch.n, (name, ch.emitted, ch.n)
    return nc


_NC_CACHE = {}


def kernel(L0_attn_x, L0_attn_w_q, L0_attn_w_k, L0_attn_w_v, L0_attn_w_o,
           L0_ffn_w_gate_up, L0_ffn_w_down, L0_c, L0_wmod):
    x = np.ascontiguousarray(np.asarray(L0_attn_x, dtype=np.float32))
    wq = np.ascontiguousarray(np.asarray(L0_attn_w_q, dtype=np.float32))
    wk = np.ascontiguousarray(np.asarray(L0_attn_w_k, dtype=np.float32))
    wv = np.ascontiguousarray(np.asarray(L0_attn_w_v, dtype=np.float32))
    wo = np.ascontiguousarray(np.asarray(L0_attn_w_o, dtype=np.float32))
    wgu = np.ascontiguousarray(np.asarray(L0_ffn_w_gate_up, dtype=np.float32))
    wd = np.ascontiguousarray(np.asarray(L0_ffn_w_down, dtype=np.float32))
    cvec = np.asarray(L0_c, dtype=np.float32)
    wmod = np.ascontiguousarray(np.asarray(L0_wmod, dtype=np.float32))

    kk = np.arange(128)[:, None]
    qq = np.arange(128)[None, :]
    mdiag = (kk > qq).astype(ml_dtypes.bfloat16)
    medge = (kk <= qq).astype(ml_dtypes.bfloat16)
    mstr = np.zeros((128, 12, 512), dtype=np.float32)
    for jj in range(12):
        for s in range(4):
            d = jj - s
            if d < 0 or d > 8:
                tile = 0.0
            elif d == 0:
                tile = (kk > qq).astype(np.float32)
            elif d == 8:
                tile = (kk <= qq).astype(np.float32)
            else:
                tile = 1.0
            mstr[:, jj, s * 128:(s + 1) * 128] = tile
    mstr = mstr.astype(ml_dtypes.bfloat16)

    if "nc" not in _NC_CACHE:
        _NC_CACHE["nc"] = build_nc()
    nc = _NC_CACHE["nc"]

    in_maps = []
    for c in range(8):
        b, hf = c // 2, c % 2
        t0 = 1024 * hf
        ctx = (np.zeros((1024, D), np.float32) if hf == 0 else x[b, 0:1024])
        xkv = np.ascontiguousarray(
            np.concatenate([ctx, x[b, t0:t0 + 1024]], axis=0))
        cv = np.ascontiguousarray(cvec[b].reshape(IC, 128).T)
        vld = np.ones((128, IC), np.float32)
        if hf == 0:
            vld[:, 0:8] = 0.0
        in_maps.append({
            "xkv": xkv,
            "wq": wq, "wk": wk, "wv": wv, "wo": wo,
            "wgu": wgu, "wd": wd, "wmod": wmod,
            "cv": cv, "valid": vld,
            "mdiag": mdiag, "medge": medge, "mstr": mstr,
        })

    res = run_bass_kernel_spmd(nc, in_maps, core_ids=list(range(8)))
    out = np.empty((4, 2048, D), np.float32)
    for c in range(8):
        b, hf = c // 2, c % 2
        out[b, 1024 * hf:1024 * hf + 1024] = res.results[c]["out"]
    return out
